# version 02 (final)
# baseline (speedup 1.0000x reference)
import contextlib
import numpy as np
import concourse.bass as bass
import concourse.mybir as mybir
from concourse.bass_utils import run_bass_kernel_spmd

F32 = mybir.dt.float32
BF16 = mybir.dt.bfloat16
U8 = mybir.dt.uint8
ALU = mybir.AluOpType
ACT = mybir.ActivationFunctionType

P = 128
DM = 2048
SEQ = 4096
TOK = 2048
MLP = 16384
KV = 512
NH, KVH = 16, 4
NCORE = 8
EPS_LN = 1e-5
EPS_ATT = 1e-6

PHASE_LIMIT = 6
DEV_SKIP = set()


def _perm(dim):
    s = np.arange(dim)
    c2, p = s // 128, s % 128
    return (256 * (c2 // 2) + 2 * p + (c2 % 2)).astype(np.int64)


PI_DM = _perm(DM)
PI_MLP = _perm(MLP)


class Sem:
    def __init__(self, stack, nc, name):
        self.h = stack.enter_context(nc.semaphore(name))
        self.v = 0
        self.name = name

    def inc(self, ins, k=1):
        self.v += k
        return ins.then_inc(self.h, k)

    def dinc(self, ins):
        return self.inc(ins, 16)


def build_program():
    nc = bass.Bass("TRN2", target_bir_lowering=False, debug=False,
                   detect_race_conditions=False)

    def din(name, shape, dt):
        return nc.dram_tensor(name, list(shape), dt, kind="ExternalInput").ap()

    x_full = din("x_full", [SEQ, DM], F32)
    wqT = din("wqT", [DM, DM], F32)
    wkT = din("wkT", [DM, KV], F32)
    wvT = din("wvT", [DM, KV], F32)
    woT = din("woT", [DM, DM], F32)
    bq_cols = din("bq_cols", [P, NH], F32)
    bk_rep = din("bk_rep", [P, KV], F32)
    bv_rep = din("bv_rep", [P, KV], F32)
    bt_g = din("bt_g", [1024, MLP], U8)
    bt_u = din("bt_u", [1024, MLP], U8)
    bt_d = din("bt_d", [8192, DM], U8)
    sc_g = din("sc_g", [1024, MLP], F32)
    sc_u = din("sc_u", [1024, MLP], F32)
    sc_d = din("sc_d", [8192, DM], F32)
    ident = din("ident_f32", [P, P], F32)
    eh_in = din("eh_onehot", [16, NH * P], BF16)

    out = nc.dram_tensor("out", [TOK, DM], F32, kind="ExternalOutput").ap()

    xn_s = nc.dram_tensor("xn_s", [SEQ, DM], BF16, kind="Internal").ap()
    xn2_s = nc.dram_tensor("xn2_s", [TOK, DM], BF16, kind="Internal").ap()
    wq_s = nc.dram_tensor("wq_s", [DM, DM], BF16, kind="Internal").ap()
    wo_s = nc.dram_tensor("wo_s", [DM, DM], BF16, kind="Internal").ap()
    inter_s = nc.dram_tensor("inter_s", [MLP, TOK], BF16, kind="Internal").ap()

    st = contextlib.ExitStack()
    with st:
        SS = lambda name: Sem(st, nc, name)
        s_ld = SS("s_ld"); s_cast = SS("s_cast"); s_cst = SS("s_cst")
        s_p0 = SS("s_p0")
        s_x1 = SS("s_x1"); s_st1 = SS("s_st1"); s_var1 = SS("s_var1")
        s_sd1 = SS("s_sd1"); s_fv1 = SS("s_fv1"); s_nm1 = SS("s_nm1")
        s_xn = SS("s_xn")
        s_xnt2 = SS("s_xnt2"); s_kmm = SS("s_kmm"); s_eluq = SS("s_eluq")
        s_exp2 = SS("s_exp2"); s_elu1 = SS("s_elu1"); s_skrdy = SS("s_skrdy")
        s_stat = SS("s_stat"); s_fv2 = SS("s_fv2"); s_p2 = SS("s_p2")
        s_x3 = SS("s_x3"); s_wq = SS("s_wq"); s_wo = SS("s_wo")
        s_xnt3 = SS("s_xnt3"); s_qmm = SS("s_qmm"); s_qt2 = SS("s_qt2")
        s_qexp = SS("s_qexp"); s_qps = SS("s_qps"); s_sqrdy = SS("s_sqrdy")
        s_nrmm = SS("s_nrmm"); s_fv3 = SS("s_fv3"); s_recip = SS("s_recip")
        s_bc = SS("s_bc"); s_sqn = SS("s_sqn"); s_qkv = SS("s_qkv")
        s_attev = SS("s_attev"); s_omm = SS("s_omm"); s_yev = SS("s_yev")
        s_yst = SS("s_yst")
        s_st2 = SS("s_st2"); s_var2 = SS("s_var2"); s_sd2 = SS("s_sd2")
        s_fv4 = SS("s_fv4"); s_nm2 = SS("s_nm2"); s_xn2 = SS("s_xn2")
        s_ph = SS("s_ph")
        s_byt = SS("s_byt"); s_scl = SS("s_scl")
        s_d1 = SS("s_d1"); s_d2 = SS("s_d2"); s_d3 = SS("s_d3"); s_d4 = SS("s_d4")
        s_x2t = SS("s_x2t"); s_gmm = SS("s_gmm"); s_gel = SS("s_gel")
        s_upc = SS("s_upc"); s_int = SS("s_int"); s_ist = SS("s_ist")
        s_ild = SS("s_ild"); s_dmm = SS("s_dmm"); s_acc = SS("s_acc")
        s_yld = SS("s_yld"); s_tp = SS("s_tp"); s_fin = SS("s_fin")
        s_ost = SS("s_ost"); s_wfree = SS("s_wfree")

        sb_eps = st.enter_context(nc.sbuf_tensor("c_eps", [P, 1], F32))
        sb_cs2 = st.enter_context(nc.sbuf_tensor("c_cs2", [P, 1], F32))
        sb_cb4 = st.enter_context(nc.sbuf_tensor("c_cb4", [P, 1], F32))
        sb_cb8 = st.enter_context(nc.sbuf_tensor("c_cb8", [P, 1], F32))
        sb_half = st.enter_context(nc.sbuf_tensor("c_half", [P, 1], F32))
        cs4 = st.enter_context(nc.sbuf_tensor("c_cs4", [P, 1], F32))
        cb16 = st.enter_context(nc.sbuf_tensor("c_cb16", [P, 1], F32))
        sb_ident = st.enter_context(nc.sbuf_tensor("c_ident", [P, P], F32))

        pb = [st.enter_context(nc.psum_tensor(f"pb{i}", [P, 512], F32))
              for i in range(8)]

        phase_round = [0]

        def barrier(block):
            phase_round[0] += 1
            target = 5 * phase_round[0]
            for ename in ("sync", "scalar", "vector", "tensor", "gpsimd"):
                def f(eng):
                    s_ph.inc(eng.nop(), 1)
                    eng.wait_ge(s_ph.h, target)
                getattr(block, ename)(f)
            assert s_ph.v == target

        stA = st.enter_context(contextlib.ExitStack())

        def sbA(name, shape, dt):
            return stA.enter_context(nc.sbuf_tensor(name, list(shape), dt))

        xbuf = sbA("xbuf", [P, 4 * DM], F32)
        ybuf = sbA("ybuf", [P, 4 * DM], F32)
        lnsc = sbA("lnsc", [P, DM], F32)
        xnb = sbA("xnb", [P, 4 * DM], BF16)
        l_sum = sbA("l_sum", [P, 1], F32)
        l_ssq = sbA("l_ssq", [P, 1], F32)
        l_mean = sbA("l_mean", [P, 1], F32)
        l_msq = sbA("l_msq", [P, 1], F32)
        l_m2t = sbA("l_m2t", [P, 1], F32)
        l_var = sbA("l_var", [P, 1], F32)
        l_sd = sbA("l_sd", [P, 1], F32)
        l_rstd = sbA("l_rstd", [P, 1], F32)
        m_sum = sbA("m_sum", [P, 1], F32)
        m_ssq = sbA("m_ssq", [P, 1], F32)
        m_mean = sbA("m_mean", [P, 1], F32)
        m_msq = sbA("m_msq", [P, 1], F32)
        m_m2t = sbA("m_m2t", [P, 1], F32)
        m_var = sbA("m_var", [P, 1], F32)
        m_sd = sbA("m_sd", [P, 1], F32)
        m_rstd = sbA("m_rstd", [P, 1], F32)
        wkb = sbA("wkb", [P, 16 * KV], BF16)
        wvb = sbA("wvb", [P, 16 * KV], BF16)
        bkr = sbA("bkr", [P, KV], F32)
        bvr = sbA("bvr", [P, KV], F32)
        bqc = sbA("bqc", [P, NH], F32)
        xnt0 = sbA("xnt0", [P, 16 * 512], BF16)
        skb = [sbA(f"skb{i}", [P, KV], BF16) for i in range(2)]
        svb = [sbA(f"svb{i}", [P, 528], BF16) for i in range(2)]
        e_t1 = sbA("e_t1", [P, KV], F32)
        e_t2 = sbA("e_t2", [P, KV], F32)
        e_e = sbA("e_e", [P, KV], F32)
        e_r = sbA("e_r", [P, KV], F32)
        ones1 = sbA("ones1", [P, 1], BF16)
        kvb = sbA("kvb", [P, KVH * P], BF16)
        ksumf = sbA("ksumf", [P, KVH], F32)
        KS = sbA("KS", [P, NH * 16], BF16)
        Eh = sbA("Eh", [16, NH * P], BF16)
        wofs = [wkb, wvb]
        q_t2, q_e, q_r = e_t2, e_e, e_r
        xn2b = xnb
        stP0 = contextlib.ExitStack()
        cw = [stP0.enter_context(nc.sbuf_tensor(f"cw{i}", [P, DM], F32))
              for i in range(2)]
        cwb = [stP0.enter_context(nc.sbuf_tensor(f"cwb{i}", [P, DM], BF16))
               for i in range(2)]

        with nc.Block("p0") as block:
            @block.sync
            def _(sync):
                s_ld.dinc(sync.dma_start(out=bkr[:], in_=bk_rep[:]))
                s_ld.dinc(sync.dma_start(out=bvr[:], in_=bv_rep[:]))
                s_ld.dinc(sync.dma_start(out=bqc[:], in_=bq_cols[:]))
                s_ld.dinc(sync.dma_start(out=sb_ident[:], in_=ident[:]))
                s_ld.dinc(sync.dma_start(out=Eh[:], in_=eh_in[:]))
                s_ld.dinc(sync.dma_start(
                    out=xbuf[:, 0:16 * KV].rearrange("p (c q) -> p c q", q=KV),
                    in_=wkT[:].rearrange("(c p) q -> p c q", p=P)))
                s_ld.dinc(sync.dma_start(
                    out=ybuf[:, 0:16 * KV].rearrange("p (c q) -> p c q", q=KV),
                    in_=wvT[:].rearrange("(c p) q -> p c q", p=P)))
                for i in range(32):
                    src = wqT if i < 16 else woT
                    c = i % 16
                    if i >= 2:
                        sync.wait_ge(s_cast.h, i - 1)
                    s_ld.dinc(sync.dma_start(out=cw[i % 2][:],
                                             in_=src[P * c:P * (c + 1), :]))
                    if i >= 1:
                        j = i - 1
                        dstj = wq_s if j < 16 else wo_s
                        cj = j % 16
                        sync.wait_ge(s_cast.h, j + 1)
                        s_cst.dinc(sync.dma_start(
                            out=dstj[P * cj:P * (cj + 1), :], in_=cwb[j % 2][:]))
                sync.wait_ge(s_cast.h, 32)
                s_cst.dinc(sync.dma_start(out=wo_s[P * 15:P * 16, :],
                                          in_=cwb[31 % 2][:]))
                sync.wait_ge(s_cst.h, s_cst.v)

            @block.vector
            def _(v):
                v.memset(sb_eps[:], EPS_LN)
                v.memset(sb_cs2[:], 2.0)
                v.memset(sb_cb4[:], -4.0)
                v.memset(sb_cb8[:], -8.0)
                v.memset(sb_half[:], 0.5)
                v.memset(cs4[:], 4.0)
                v.memset(cb16[:], -16.0)
                v.memset(ones1[:], 1.0)
                v.memset(KS[:], 0.0)
                for i in range(2):
                    for kvh in range(KVH):
                        v.memset(svb[i][:, 132 * kvh + 128:132 * kvh + 129], 1.0)
                v.wait_ge(s_ld.h, 7 * 16)
                v.tensor_copy(wkb[:], xbuf[:, 0:16 * KV])
                s_p0.inc(v.tensor_copy(wvb[:], ybuf[:, 0:16 * KV]))
                for i in range(32):
                    v.wait_ge(s_ld.h, (8 + i) * 16)
                    if i >= 2:
                        v.wait_ge(s_cst.h, 16 * (i - 1))
                    s_cast.inc(v.tensor_copy(cwb[i % 2][:], cw[i % 2][:]))

        stP0.close()
        wqcol = [sbA(f"wqcol{i}", [P, 16 * P], BF16) for i in range(2)]
        sqb = sbA("sqb", [P, NH * 512], BF16)
        sqnb = [sbA(f"sqnb{i}", [P, 512], BF16) for i in range(2)]
        nrm_f = sbA("nrm_f", [16, 512], F32)
        nrm_r = sbA("nrm_r", [16, 512], F32)
        nrm_b = sbA("nrm_b", [16, 512], BF16)
        attnT = sbA("attnT", [P, NH * 512], BF16)

        NT1 = 8
        with nc.Block("p1") as block:
            @block.sync
            def _(sync):
                for t in range(NT1):
                    if t >= 2:
                        sync.wait_ge(s_nm1.h, 4 * (t - 1))
                    s_x1.dinc(sync.dma_start(
                        out=(xbuf if t % 2 == 0 else ybuf)[:]
                            .rearrange("p (s q) -> p s q", q=DM),
                        in_=x_full[512 * t:512 * (t + 1), :]
                            .rearrange("(s p) q -> p s q", p=P)))
                    if t >= 1:
                        j = t - 1
                        sync.wait_ge(s_nm1.h, 4 * (j + 1))
                        s_xn.dinc(sync.dma_start(
                            out=xn_s[512 * j:512 * (j + 1), :]
                                .rearrange("(s p) q -> p s q", p=P),
                            in_=xnb[:].rearrange("p (s q) -> p s q", q=DM)))
                sync.wait_ge(s_nm1.h, 4 * NT1)
                s_xn.dinc(sync.dma_start(
                    out=xn_s[512 * (NT1 - 1):512 * NT1, :]
                        .rearrange("(s p) q -> p s q", p=P),
                    in_=xnb[:].rearrange("p (s q) -> p s q", q=DM)))
                sync.wait_ge(s_xn.h, s_xn.v)

            @block.scalar
            def _(sc):
                for t in range(NT1):
                    buf = xbuf if t % 2 == 0 else ybuf
                    for s in range(4):
                        i = 4 * t + s
                        sc.wait_ge(s_x1.h, 16 * (t + 1))
                        xs = buf[:, DM * s:DM * (s + 1)]
                        sc.activation(out=lnsc[:], in_=xs, func=ACT.Square,
                                      accum_out=l_ssq[:])
                        s_st1.inc(sc.activation(out=lnsc[:], in_=xs, func=ACT.Copy,
                                                accum_out=l_sum[:]))
                        sc.wait_ge(s_var1.h, i + 1)
                        s_sd1.inc(sc.activation(out=l_sd[:], in_=l_var[:],
                                                func=ACT.Sqrt, bias=sb_eps[:]))

            @block.vector
            def _(v):
                for t in range(NT1):
                    buf = xbuf if t % 2 == 0 else ybuf
                    for s in range(4):
                        i = 4 * t + s
                        v.wait_ge(s_st1.h, i + 1)
                        if s == 0 and t >= 1:
                            v.wait_ge(s_xn.h, 16 * t)
                        v.tensor_scalar(l_mean[:], l_sum[:], 1.0 / DM, None, ALU.mult)
                        s_fv1.inc(v.tensor_scalar(l_msq[:], l_ssq[:], 1.0 / DM,
                                                  None, ALU.mult))
                        v.wait_ge(s_fv1.h, 3 * i + 1)
                        s_fv1.inc(v.tensor_mul(l_m2t[:], l_mean[:], l_mean[:]))
                        v.wait_ge(s_fv1.h, 3 * i + 2)
                        s_var1.inc(v.tensor_sub(l_var[:], l_msq[:], l_m2t[:]))
                        v.wait_ge(s_sd1.h, i + 1)
                        s_fv1.inc(v.reciprocal(l_rstd[:], l_sd[:]))
                        v.wait_ge(s_fv1.h, 3 * i + 3)
                        xs = buf[:, DM * s:DM * (s + 1)]
                        s_nm1.inc(v.tensor_scalar(
                            xnb[:, DM * s:DM * (s + 1)], xs, l_mean[:], l_rstd[:],
                            ALU.subtract, ALU.mult))

        assert s_nm1.v == 32 and s_st1.v == 32 and s_fv1.v == 96

        if PHASE_LIMIT >= 2:
            with nc.Block("p2") as block:
                @block.scalar
                def _(sc):
                    for t in range(NT1):
                        sc.wait_ge(s_xn.h, 16 * (t + 1))
                        if t >= 1:
                            sc.wait_ge(s_stat.h, 4 * t)
                        for fc in range(16):
                            s_xnt2.dinc(sc.dma_start(
                                out=xnt0[:, 512 * fc:512 * (fc + 1)],
                                in_=xn_s[512 * t:512 * (t + 1), P * fc:P * (fc + 1)],
                                transpose=True))
                        for s in range(4):
                            gs = 4 * t + s
                            sc.wait_ge(s_eluq.h, gs + 1)
                            if gs >= 1:
                                sc.wait_ge(s_skrdy.h, gs)
                            s_exp2.inc(sc.activation(out=e_e[:], in_=e_t2[:],
                                                     func=ACT.Exp))

                @block.tensor
                def _(tn):
                    tn.wait_ge(s_p0.h, 1)
                    for t in range(NT1):
                        tn.wait_ge(s_xnt2.h, 256 * (t + 1))
                        for s in range(4):
                            gs = 4 * t + s
                            par = gs % 2
                            if gs >= 2:
                                tn.wait_ge(s_elu1.h, gs - 1)
                            for fc in range(16):
                                tn.matmul(pb[0 + par][:],
                                          xnt0[:, 512 * fc + P * s:512 * fc + P * (s + 1)],
                                          wkb[:, 512 * fc:512 * (fc + 1)],
                                          start=(fc == 0), stop=(fc == 15))
                            for fc in range(16):
                                ins = tn.matmul(pb[2 + par][:],
                                                xnt0[:, 512 * fc + P * s:512 * fc + P * (s + 1)],
                                                wvb[:, 512 * fc:512 * (fc + 1)],
                                                start=(fc == 0), stop=(fc == 15))
                            s_kmm.inc(ins)
                            tn.wait_ge(s_elu1.h, gs + 1)
                            tn.wait_ge(s_skrdy.h, gs + 1)
                            for kvh in range(KVH):
                                ins = tn.matmul(pb[4 + kvh][:, 0:P + 1],
                                                skb[par][:, P * kvh:P * (kvh + 1)],
                                                svb[par][:, 132 * kvh:132 * kvh + P + 1],
                                                start=(gs == 0), stop=(gs == 31))
                            s_stat.inc(ins)

                @block.vector
                def _(v):
                    for t in range(NT1):
                        for s in range(4):
                            gs = 4 * t + s
                            par = gs % 2
                            v.wait_ge(s_kmm.h, gs + 1)
                            if gs >= 2:
                                v.wait_ge(s_stat.h, gs - 1)
                            v.tensor_add(e_t1[:], pb[0 + par][:], bkr[:])
                            s_eluq.inc(v.tensor_scalar(e_t2[:], e_t1[:], 0.0,
                                                       None, ALU.min))
                            v.tensor_scalar(e_r[:], e_t1[:], 0.0, None, ALU.max)
                            s_elu1.inc(v.tensor_add(
                                svb[par][:, 0:528].rearrange(
                                    "p (k d) -> p k d", d=132)[:, :, 0:P],
                                pb[2 + par][:].rearrange("p (k d) -> p k d", d=P),
                                bvr[:].rearrange("p (k d) -> p k d", d=P)))
                            v.wait_ge(s_exp2.h, gs + 1)
                            s_skrdy.inc(v.tensor_add(skb[par][:], e_e[:], e_r[:]))
                    v.wait_ge(s_stat.h, 32)
                    for kvh in range(KVH):
                        v.tensor_copy(kvb[:, P * kvh:P * (kvh + 1)],
                                      pb[4 + kvh][:, 0:P])
                        ins = v.tensor_copy(ksumf[:, kvh:kvh + 1],
                                            pb[4 + kvh][:, P:P + 1])
                    s_fv2.inc(ins)
                    v.wait_ge(s_fv2.h, 1)
                    for h in range(NH):
                        ins = v.tensor_copy(KS[:, 16 * h + h:16 * h + h + 1],
                                            ksumf[:, h // 4:h // 4 + 1])
                    s_p2.inc(ins)
            assert s_kmm.v == 32 and s_stat.v == 32 and s_skrdy.v == 32

        if PHASE_LIMIT >= 3:
            NT3 = 4
            with nc.Block("p3") as block:
                @block.sync
                def _(sync):
                    sync.wait_ge(s_nm1.h, 32)
                    if PHASE_LIMIT >= 2:
                        sync.wait_ge(s_stat.h, 32)
                    for t in range(NT3):
                        if t >= 1:
                            sync.wait_ge(s_yev.h, 16 * t)
                        s_x3.dinc(sync.dma_start(
                            out=xbuf[:].rearrange("p (s q) -> p s q", q=DM),
                            in_=x_full[512 * t:512 * (t + 1), :]
                                .rearrange("(s p) q -> p s q", p=P)))
                        if t >= 1:
                            j = t - 1
                            sync.wait_ge(s_yev.h, 16 * (j + 1))
                            s_yst.dinc(sync.dma_start(
                                out=out[512 * j:512 * (j + 1), :]
                                    .rearrange("(s p) q -> p s q", p=P),
                                in_=ybuf[:].rearrange("p (s q) -> p s q", q=DM)))
                            sync.wait_ge(s_nm2.h, 4 * (j + 1))
                            s_xn2.dinc(sync.dma_start(
                                out=xn2_s[512 * j:512 * (j + 1), :]
                                    .rearrange("(s p) q -> p s q", p=P),
                                in_=xn2b[:].rearrange("p (s q) -> p s q", q=DM)))
                        for h in range(NH):
                            i = t * NH + h
                            if i >= 2:
                                sync.wait_ge(s_qmm.h, i - 1)
                            s_wq.dinc(sync.dma_start(
                                out=wqcol[i % 2][:].rearrange("p (c r) -> p c r", r=P),
                                in_=wq_s[:, P * h:P * (h + 1)]
                                    .rearrange("(c p) r -> p c r", p=P)))
                        for fs in range(4):
                            i = t * 4 + fs
                            if i >= 2:
                                sync.wait_ge(s_omm.h, 4 * (i - 1))
                            s_wo.dinc(sync.dma_start(
                                out=wofs[i % 2][:].rearrange("p (c r) -> p c r", r=512),
                                in_=wo_s[:, 512 * fs:512 * (fs + 1)]
                                    .rearrange("(c p) r -> p c r", p=P)))
                    j = NT3 - 1
                    sync.wait_ge(s_yev.h, 16 * (j + 1))
                    s_yst.dinc(sync.dma_start(
                        out=out[512 * j:512 * (j + 1), :]
                            .rearrange("(s p) q -> p s q", p=P),
                        in_=ybuf[:].rearrange("p (s q) -> p s q", q=DM)))
                    sync.wait_ge(s_nm2.h, 4 * (j + 1))
                    s_xn2.dinc(sync.dma_start(
                        out=xn2_s[512 * j:512 * (j + 1), :]
                            .rearrange("(s p) q -> p s q", p=P),
                        in_=xn2b[:].rearrange("p (s q) -> p s q", q=DM)))
                    sync.wait_ge(s_yst.h, s_yst.v)
                    sync.wait_ge(s_xn2.h, s_xn2.v)

                @block.scalar
                def _(sc):
                    for t in range(NT3):
                        if t == 0:
                            sc.wait_ge(s_xn.h, 16 * NT1)
                        if t == 0 and PHASE_LIMIT >= 2:
                            sc.wait_ge(s_stat.h, 32)
                        if t >= 1:
                            sc.wait_ge(s_nrmm.h, t)
                        for fc in range(16):
                            s_xnt3.dinc(sc.dma_start(
                                out=xnt0[:, 512 * fc:512 * (fc + 1)],
                                in_=xn_s[512 * t:512 * (t + 1), P * fc:P * (fc + 1)],
                                transpose=True))
                        for h in range(NH):
                            gh = t * NH + h
                            sc.wait_ge(s_qt2.h, gh + 1)
                            if gh >= 1:
                                sc.wait_ge(s_sqrdy.h, gh)
                            s_qexp.inc(sc.activation(out=q_e[:], in_=q_t2[:],
                                                     func=ACT.Exp))
                        for s in range(4):
                            i = 4 * t + s
                            sc.wait_ge(s_yev.h, 16 * (t + 1))
                            ys = ybuf[:, DM * s:DM * (s + 1)]
                            sc.activation(out=lnsc[:], in_=ys, func=ACT.Square,
                                          accum_out=m_ssq[:])
                            s_st2.inc(sc.activation(out=lnsc[:], in_=ys, func=ACT.Copy,
                                                    accum_out=m_sum[:]))
                            sc.wait_ge(s_var2.h, i + 1)
                            s_sd2.inc(sc.activation(out=m_sd[:], in_=m_var[:],
                                                    func=ACT.Sqrt, bias=sb_eps[:]))

                @block.tensor
                def _(tn):
                    for t in range(NT3):
                        tn.wait_ge(s_xnt3.h, 256 * (t + 1))
                        if PHASE_LIMIT >= 2:
                            tn.wait_ge(s_p2.h, 1)
                        if t >= 1:
                            tn.wait_ge(s_recip.h, t)
                            tn.wait_ge(s_yev.h, 16 * t)
                        for h in range(NH):
                            gh = t * NH + h
                            tn.wait_ge(s_wq.h, 16 * (gh + 1))
                            if gh >= 2:
                                tn.wait_ge(s_qps.h, gh - 1)
                            for fc in range(16):
                                ins = tn.matmul(pb[0 + h % 2][:],
                                                wqcol[gh % 2][:, P * fc:P * (fc + 1)],
                                                xnt0[:, 512 * fc:512 * (fc + 1)],
                                                start=(fc == 0), stop=(fc == 15))
                            s_qmm.inc(ins)
                            tn.wait_ge(s_sqrdy.h, gh + 1)
                            ins = tn.matmul(pb[4][:16, :],
                                            KS[:, 16 * h:16 * (h + 1)],
                                            sqb[:, 512 * h:512 * (h + 1)],
                                            start=(h == 0), stop=(h == 15))
                        s_nrmm.inc(ins)
                        tn.wait_ge(s_recip.h, t + 1)
                        for h in range(NH):
                            gh = t * NH + h
                            if gh >= 2:
                                tn.wait_ge(s_sqn.h, gh - 1)
                            s_bc.inc(tn.matmul(pb[2 + h % 2][:],
                                               Eh[:, P * h:P * (h + 1)],
                                               nrm_b[:], start=True, stop=True))
                            if gh >= 2:
                                tn.wait_ge(s_attev.h, gh - 1)
                            tn.wait_ge(s_sqn.h, gh + 1)
                            kvh = h // 4
                            s_qkv.inc(tn.matmul(pb[6 + h % 2][:],
                                                kvb[:, P * kvh:P * (kvh + 1)],
                                                sqnb[h % 2][:], start=True, stop=True))
                        tn.wait_ge(s_attev.h, NH * (t + 1))
                        for fs in range(4):
                            gf = t * 4 + fs
                            tn.wait_ge(s_wo.h, 16 * (gf + 1))
                            for tsub in range(4):
                                idx = t * 16 + fs * 4 + tsub
                                if idx >= 2:
                                    tn.wait_ge(s_yev.h, idx - 1)
                                for h in range(NH):
                                    ins = tn.matmul(pb[4 + idx % 2][:],
                                                    attnT[:, 512 * h + P * tsub:
                                                          512 * h + P * (tsub + 1)],
                                                    wofs[gf % 2][:, 512 * h:512 * (h + 1)],
                                                    start=(h == 0), stop=(h == 15))
                                s_omm.inc(ins)

                @block.vector
                def _(v):
                    for t in range(NT3):
                        if t >= 1:
                            v.wait_ge(s_nrmm.h, t)
                            v.wait_ge(s_omm.h, 16 * t)
                        for h in range(NH):
                            gh = t * NH + h
                            v.wait_ge(s_qmm.h, gh + 1)
                            bq_ap = bqc[:, h:h + 1]
                            s_qt2.inc(v.tensor_scalar(q_t2[:], pb[0 + h % 2][:],
                                                      bq_ap, 0.0, ALU.add, ALU.min))
                            s_qps.inc(v.tensor_scalar(q_r[:], pb[0 + h % 2][:],
                                                      bq_ap, 0.0, ALU.add, ALU.max))
                            v.wait_ge(s_qexp.h, gh + 1)
                            s_sqrdy.inc(v.tensor_add(sqb[:, 512 * h:512 * (h + 1)],
                                                     q_e[:], q_r[:]))
                        v.wait_ge(s_nrmm.h, t + 1)
                        s_fv3.inc(v.tensor_scalar(nrm_f[:], pb[4][:16, :], EPS_ATT,
                                                  None, ALU.add))
                        v.wait_ge(s_fv3.h, 3 * t + 1)
                        s_fv3.inc(v.reciprocal(nrm_r[:], nrm_f[:]))
                        v.wait_ge(s_fv3.h, 3 * t + 2)
                        s_fv3.inc(v.tensor_copy(nrm_b[:], nrm_r[:]))
                        v.wait_ge(s_fv3.h, 3 * t + 3)
                        s_recip.inc(v.nop())
                        for h in range(NH):
                            gh = t * NH + h
                            v.wait_ge(s_bc.h, gh + 1)
                            if gh >= 2:
                                v.wait_ge(s_qkv.h, gh - 1)
                            s_sqn.inc(v.tensor_mul(sqnb[h % 2][:],
                                                   sqb[:, 512 * h:512 * (h + 1)],
                                                   pb[2 + h % 2][:]))
                            v.wait_ge(s_qkv.h, gh + 1)
                            s_attev.inc(v.tensor_copy(
                                attnT[:, 512 * h:512 * (h + 1)], pb[6 + h % 2][:]))
                        if t >= 1:
                            v.wait_ge(s_yst.h, 16 * t)
                            v.wait_ge(s_nm2.h, 4 * t)
                        v.wait_ge(s_x3.h, 16 * (t + 1))
                        for fs in range(4):
                            for tsub in range(4):
                                idx = t * 16 + fs * 4 + tsub
                                v.wait_ge(s_omm.h, idx + 1)
                                s_yev.inc(v.tensor_add(
                                    ybuf[:, DM * tsub + 512 * fs:
                                         DM * tsub + 512 * (fs + 1)],
                                    pb[4 + idx % 2][:],
                                    xbuf[:, DM * tsub + 512 * fs:
                                         DM * tsub + 512 * (fs + 1)]))
                        for s in range(4):
                            i = 4 * t + s
                            v.wait_ge(s_st2.h, i + 1)
                            if s == 0 and t >= 1:
                                v.wait_ge(s_xn2.h, 16 * t)
                            v.tensor_scalar(m_mean[:], m_sum[:], 1.0 / DM, None,
                                            ALU.mult)
                            s_fv4.inc(v.tensor_scalar(m_msq[:], m_ssq[:], 1.0 / DM,
                                                      None, ALU.mult))
                            v.wait_ge(s_fv4.h, 3 * i + 1)
                            s_fv4.inc(v.tensor_mul(m_m2t[:], m_mean[:], m_mean[:]))
                            v.wait_ge(s_fv4.h, 3 * i + 2)
                            s_var2.inc(v.tensor_sub(m_var[:], m_msq[:], m_m2t[:]))
                            v.wait_ge(s_sd2.h, i + 1)
                            s_fv4.inc(v.reciprocal(m_rstd[:], m_sd[:]))
                            v.wait_ge(s_fv4.h, 3 * i + 3)
                            ys = ybuf[:, DM * s:DM * (s + 1)]
                            s_nm2.inc(v.tensor_scalar(
                                xn2b[:, DM * s:DM * (s + 1)], ys, m_mean[:],
                                m_rstd[:], ALU.subtract, ALU.mult))

            assert s_qmm.v == 64 and s_omm.v == 64 and s_yev.v == 64

        if PHASE_LIMIT >= 4:
            stA.close()
            stB = st.enter_context(contextlib.ExitStack())

            def sbB(name, shape, dt):
                return stB.enter_context(nc.sbuf_tensor(name, list(shape), dt))

            bb = [sbB(f"bb{i}", [P, 512], U8) for i in range(2)]
            cat = [sbB(f"cat{i}", [P, 2048], U8) for i in range(2)]
            cf = [sbB(f"cf{i}", [P, 2048], BF16) for i in range(2)]
            sgm = [sbB(f"sgm{i}", [P, 1024], BF16) for i in range(2)]
            u1b = [sbB(f"u1b{i}", [P, 1024], BF16) for i in range(2)]
            u2b = [sbB(f"u2b{i}", [P, 1024], BF16) for i in range(2)]
            Sf = [sbB(f"Sf{i}", [P, 512], F32) for i in range(2)]
            SbD = [sbB(f"SbD{i}", [P, 1024], BF16) for i in range(2)]
            m2b = [sbB(f"m2b{i}", [P, 1024], BF16) for i in range(2)]
            sSb = [sbB(f"sSb{i}", [P, 1024], BF16) for i in range(2)]
            m3b = [sbB(f"m3b{i}", [P, 1024], BF16) for i in range(2)]
            wbuf = [[sbB(f"wbuf{m}{i}", [P, 16 * 512], BF16) for i in range(2)]
                    for m in range(2)]

            DT = [0]

            def emit_decode_sync(sync, bsrc, ssrc, r0, c0):
                i = DT[0]
                if i >= 2:
                    sync.wait_ge(s_d1.h, i - 1)
                    sync.wait_ge(s_d2.h, i - 1)
                s_byt.dinc(sync.dma_start(out=bb[i % 2][:],
                                          in_=bsrc[r0:r0 + P, c0:c0 + 512]))
                s_scl.dinc(sync.dma_start(out=Sf[i % 2][:],
                                          in_=ssrc[r0:r0 + P, c0:c0 + 512]))

            def emit_decode_d1(v, i):
                par = i % 2
                v.wait_ge(s_byt.h, 16 * (i + 1))
                if i >= 2:
                    v.wait_ge(s_d2.h, i - 1)
                    v.wait_ge(s_d4.h, i - 1)
                v.tensor_scalar(cat[par][:, 0:512], bb[par][:], 7, None,
                                ALU.bitwise_and)
                v.tensor_scalar(cat[par][:, 512:1024], bb[par][:], 4, 7,
                                ALU.logical_shift_right, ALU.bitwise_and)
                v.tensor_scalar(cat[par][:, 1024:1536], bb[par][:], 8, None,
                                ALU.bitwise_and)
                v.tensor_scalar(cat[par][:, 1536:2048], bb[par][:], 4, 8,
                                ALU.logical_shift_right, ALU.bitwise_and)
                v.tensor_copy(cf[par][:], cat[par][:])
                s_d1.inc(v.tensor_scalar(sgm[par][:], cf[par][:, 1024:2048],
                                         -0.25, 1.0, ALU.mult, ALU.add))

            def emit_decode_d2(sc, i):
                par = i % 2
                sc.wait_ge(s_d1.h, i + 1)
                if i >= 2:
                    sc.wait_ge(s_d3.h, i - 1)
                    sc.wait_ge(s_d4.h, i - 1)
                sc.activation(out=u1b[par][:], in_=cf[par][:, 0:1024],
                              func=ACT.Identity, bias=sb_cb4[:], scale=sb_cs2[:])
                sc.activation(out=u2b[par][:], in_=cf[par][:, 0:1024],
                              func=ACT.Identity, bias=cb16[:], scale=cs4[:])
                sc.activation(out=SbD[par][:, 0:512], in_=Sf[par][:],
                              func=ACT.Identity, scale=sb_half[:])
                s_d2.inc(sc.activation(out=SbD[par][:, 512:1024], in_=Sf[par][:],
                                       func=ACT.Identity, scale=sb_half[:]))

            def emit_decode_d3(v, i):
                par = i % 2
                v.wait_ge(s_d2.h, i + 1)
                if i >= 2:
                    v.wait_ge(s_d4.h, i - 1)
                v.tensor_max(m2b[0][:], u1b[par][:], u2b[par][:])
                s_d3.inc(v.tensor_max(m3b[par][:], m2b[0][:],
                                      cf[par][:, 0:1024]))

            def emit_decode_d4(g, i, wdst, wfree):
                par = i % 2
                g.wait_ge(s_d3.h, i + 1)
                if wfree is not None:
                    g.wait_ge(wfree[0].h, wfree[1])
                g.tensor_mul(sSb[0][:], sgm[par][:], SbD[par][:])
                s_d4.inc(g.tensor_mul(wdst, m3b[par][:], sSb[0][:]))

            stP4 = stB.enter_context(contextlib.ExitStack())
            x2t = stP4.enter_context(nc.sbuf_tensor("x2t", [P, 16 * 2048], BF16))
            intb = [stP4.enter_context(nc.sbuf_tensor(f"intb{i}", [P, 2048], BF16))
                    for i in range(2)]
            ggb = [stP4.enter_context(nc.sbuf_tensor(f"ggb{i}", [P, 512], BF16))
                   for i in range(2)]
            uub = [stP4.enter_context(nc.sbuf_tensor(f"uub{i}", [P, 512], BF16))
                   for i in range(2)]

            NG = 32

            def p4_tile_params(i):
                g, r = divmod(i, 16)
                m, c = divmod(r, 8)
                bsrc = bt_g if m == 0 else bt_u
                ssrc = sc_g if m == 0 else sc_u
                return g, m, c, bsrc, ssrc

            with nc.Block("p4") as block:
                @block.scalar
                def _(sc):
                    sc.wait_ge(s_xn2.h, 64)
                    for fc in range(16):
                        s_x2t.dinc(sc.dma_start(
                            out=x2t[:, 2048 * fc:2048 * (fc + 1)],
                            in_=xn2_s[:, P * fc:P * (fc + 1)], transpose=True))
                    for g in range(NG):
                        for j in range(16):
                            emit_decode_d2(sc, g * 16 + j)
                            if g >= 1:
                                eidx = (g - 1) * 16 + j
                                ts = j % 4
                                sc.wait_ge(s_gmm.h, eidx + 1)
                                if eidx >= 2:
                                    sc.wait_ge(s_int.h, eidx - 1)
                                s_gel.inc(sc.activation(
                                    out=ggb[ts % 2][:], in_=pb[0 + ts % 2][:],
                                    func=ACT.Gelu_apprx_tanh))
                    for j in range(16):
                        eidx = (NG - 1) * 16 + j
                        ts = j % 4
                        sc.wait_ge(s_gmm.h, eidx + 1)
                        if eidx >= 2:
                            sc.wait_ge(s_int.h, eidx - 1)
                        s_gel.inc(sc.activation(
                            out=ggb[ts % 2][:], in_=pb[0 + ts % 2][:],
                            func=ACT.Gelu_apprx_tanh))

                @block.sync
                def _(sync):
                    sync.wait_ge(s_xn2.h, 64)
                    for g in range(NG):
                        for j in range(16):
                            _, m, c, bsrc, ssrc = p4_tile_params(g * 16 + j)
                            emit_decode_sync(sync, bsrc, ssrc, 128 * c, 512 * g)
                            DT[0] += 1
                            if g >= 1 and j % 4 == 3:
                                ncg = j // 4
                                ncc = (g - 1) * 4 + ncg
                                sync.wait_ge(s_int.h, 4 * (ncc + 1))
                                s_ist.dinc(sync.dma_start(
                                    out=inter_s[P * ncc:P * (ncc + 1), :],
                                    in_=intb[ncg % 2][:]))
                    for ncg in range(4):
                        ncc = (NG - 1) * 4 + ncg
                        sync.wait_ge(s_int.h, 4 * (ncc + 1))
                        s_ist.dinc(sync.dma_start(
                            out=inter_s[P * ncc:P * (ncc + 1), :],
                            in_=intb[ncg % 2][:]))

                @block.vector
                def _(v):
                    def d4_for(i2, wfloor):
                        emit_decode_d3(v, i2)

                    def evac(g1, j):
                        eidx = g1 * 16 + j
                        ncg, ts = divmod(j, 4)
                        ncc = g1 * 4 + ncg
                        v.wait_ge(s_gmm.h, eidx + 1)
                        if ncc >= 2:
                            v.wait_ge(s_ist.h, 16 * (ncc - 1))
                        s_upc.inc(v.tensor_copy(uub[ts % 2][:], pb[2 + ts % 2][:]))
                        v.wait_ge(s_gel.h, eidx + 1)
                        s_int.inc(v.tensor_mul(
                            intb[ncg % 2][:, 512 * ts:512 * (ts + 1)],
                            ggb[ts % 2][:], uub[ts % 2][:]))

                    for g in range(NG):
                        for j in range(16):
                            i = g * 16 + j
                            if j >= 2:
                                d4_for(i - 2, 0)
                            emit_decode_d1(v, i)
                            if g >= 1:
                                evac(g - 1, j)
                        d4_for(g * 16 + 14, 0)
                        d4_for(g * 16 + 15, 0)
                    for j in range(16):
                        evac(NG - 1, j)

                @block.gpsimd
                def _(gp):
                    for i in range(NG * 16):
                        g2, m2_, c2, _, _ = p4_tile_params(i)
                        wdst = wbuf[m2_][g2 % 2][:, 1024 * c2:1024 * (c2 + 1)]
                        emit_decode_d4(gp, i, wdst,
                                       (s_gmm, 16 * (g2 - 1)) if g2 >= 2 else None)

                @block.tensor
                def _(tn):
                    tn.wait_ge(s_x2t.h, 256)
                    for g in range(NG):
                        tn.wait_ge(s_d4.h, 16 * (g + 1))
                        for ncg in range(4):
                            for ts in range(4):
                                eidx = g * 16 + ncg * 4 + ts
                                if eidx >= 2:
                                    tn.wait_ge(s_gel.h, eidx - 1)
                                    tn.wait_ge(s_upc.h, eidx - 1)
                                for k in range(16):
                                    tn.matmul(
                                        pb[0 + ts % 2][:],
                                        wbuf[0][g % 2][:, 512 * k + P * ncg:
                                                       512 * k + P * (ncg + 1)],
                                        x2t[:, 2048 * k + 512 * ts:
                                            2048 * k + 512 * (ts + 1)],
                                        start=(k == 0), stop=(k == 15))
                                for k in range(16):
                                    ins = tn.matmul(
                                        pb[2 + ts % 2][:],
                                        wbuf[1][g % 2][:, 512 * k + P * ncg:
                                                       512 * k + P * (ncg + 1)],
                                        x2t[:, 2048 * k + 512 * ts:
                                            2048 * k + 512 * (ts + 1)],
                                        start=(k == 0), stop=(k == 15))
                                s_gmm.inc(ins)
            assert s_gmm.v == NG * 16 and s_d4.v == NG * 16
            assert s_int.v == NG * 16 and s_ist.v == 16 * NG * 4

        if PHASE_LIMIT >= 5:
            if "keep_p4" not in DEV_SKIP:
                stP4.close()
            accq = stB.enter_context(nc.sbuf_tensor("accq", [P, 4 * TOK], F32))
            ib = [stB.enter_context(nc.sbuf_tensor(f"ib{i}", [P, 16 * 512], BF16))
                  for i in range(2)]
            ybuf5 = [stB.enter_context(nc.sbuf_tensor(f"ybuf5{i}", [P, 512], F32))
                     for i in range(2)]
            finb = [stB.enter_context(nc.sbuf_tensor(f"finb{i}", [P, 512], F32))
                    for i in range(2)]

            P4_DT = NG * 16 if PHASE_LIMIT >= 4 else 0
            NB = 8

            with nc.Block("p5") as block:
                @block.sync
                def _(sync):
                    if PHASE_LIMIT >= 4:
                        sync.wait_ge(s_gmm.h, 512)
                    for fq in range(4):
                        for b in range(NB):
                            bfq = fq * NB + b
                            for jc in range(8):
                                emit_decode_sync(
                                    sync, bt_d, sc_d,
                                    1024 * b + 128 * jc, 512 * fq)
                                DT[0] += 1
                            for ts in range(4):
                                tsi = bfq * 4 + ts
                                if tsi >= 2:
                                    sync.wait_ge(s_dmm.h, 4 * (tsi - 1))
                                for ncb in range(16):
                                    s_ild.dinc(sync.dma_start(
                                        out=ib[tsi % 2][:, 512 * ncb:512 * (ncb + 1)],
                                        in_=inter_s[2048 * b + P * ncb:
                                                    2048 * b + P * (ncb + 1),
                                                    512 * ts:512 * (ts + 1)]))
                        for ti in range(16):
                            fti = fq * 16 + ti
                            if fti >= 2:
                                sync.wait_ge(s_fin.h, 4 * (fti - 1))
                            s_yld.dinc(sync.dma_start(
                                out=ybuf5[ti % 2][:],
                                in_=out[P * ti:P * (ti + 1),
                                        512 * fq:512 * (fq + 1)]))
                            sync.wait_ge(s_fin.h, 4 * (fti + 1))
                            s_ost.dinc(sync.dma_start(
                                out=out[P * ti:P * (ti + 1),
                                        512 * fq:512 * (fq + 1)],
                                in_=finb[ti % 2][:]))
                    sync.wait_ge(s_ost.h, s_ost.v)

                @block.scalar
                def _(sc):
                    for fq in range(4):
                        for b in range(NB):
                            for jc in range(8):
                                emit_decode_d2(sc, P4_DT + (fq * NB + b) * 8 + jc)

                @block.vector
                def _(v):
                    base = P4_DT

                    def d4_for5(r2):
                        emit_decode_d3(v, base + r2)

                    for fq in range(4):
                        for b in range(NB):
                            bfq = fq * NB + b
                            for jc in range(8):
                                i = base + bfq * 8 + jc
                                if jc >= 2:
                                    d4_for5(bfq * 8 + jc - 2)
                                emit_decode_d1(v, i)
                            d4_for5(bfq * 8 + 6)
                            d4_for5(bfq * 8 + 7)
                            for ts in range(4):
                                for f in range(4):
                                    didx = (bfq * 4 + ts) * 4 + f
                                    v.wait_ge(s_dmm.h, didx + 1)
                                    dst = accq[:, 2048 * f + 512 * ts:
                                               2048 * f + 512 * (ts + 1)]
                                    if b == 0:
                                        s_acc.inc(v.tensor_copy(
                                            dst, pb[4 + f % 2][:]))
                                    else:
                                        s_acc.inc(v.tensor_add(
                                            dst, dst, pb[4 + f % 2][:]))
                        for ti in range(16):
                            fti = fq * 16 + ti
                            for fc in range(4):
                                tf = fti * 4 + fc
                                v.wait_ge(s_tp.h, tf + 1)
                                v.wait_ge(s_yld.h, 16 * (fti + 1))
                                if fti >= 2:
                                    v.wait_ge(s_ost.h, 16 * (fti - 1))
                                s_fin.inc(v.tensor_add(
                                    finb[ti % 2][:, 128 * fc:128 * (fc + 1)],
                                    pb[6 + fc % 2][:, 0:P],
                                    ybuf5[ti % 2][:, 128 * fc:128 * (fc + 1)]))

                @block.gpsimd
                def _(gp):
                    for i in range(P4_DT, DT[0]):
                        r2 = i - P4_DT
                        bfq2, jc2 = divmod(r2, 8)
                        wdst = wbuf[0][bfq2 % 2][:, 1024 * jc2:1024 * (jc2 + 1)]
                        if bfq2 >= 2:
                            wft = (s_dmm, 16 * (bfq2 - 1))
                        elif PHASE_LIMIT >= 4:
                            wft = (s_gmm, 512)
                        else:
                            wft = None
                        emit_decode_d4(gp, i, wdst, wft)

                @block.tensor
                def _(tn):
                    for fq in range(4):
                        for b in range(NB):
                            bfq = fq * NB + b
                            tn.wait_ge(s_d4.h, P4_DT + 8 * (bfq + 1))
                            for ts in range(4):
                                tsi = bfq * 4 + ts
                                tn.wait_ge(s_ild.h, 16 * 16 * (tsi + 1))
                                for f in range(4):
                                    didx = tsi * 4 + f
                                    if didx >= 2:
                                        tn.wait_ge(s_acc.h, didx - 1)
                                    for ncb in range(16):
                                        ins = tn.matmul(
                                            pb[4 + f % 2][:],
                                            wbuf[0][bfq % 2][:, 512 * ncb + P * f:
                                                             512 * ncb + P * (f + 1)],
                                            ib[tsi % 2][:, 512 * ncb:512 * (ncb + 1)],
                                            start=(ncb == 0), stop=(ncb == 15))
                                    s_dmm.inc(ins)
                        tn.wait_ge(s_acc.h, 128 * (fq + 1))
                        for ti in range(16):
                            fti = fq * 16 + ti
                            for fc in range(4):
                                tf = fti * 4 + fc
                                if tf >= 2:
                                    tn.wait_ge(s_fin.h, tf - 1)
                                s_tp.inc(tn.transpose(
                                    pb[6 + fc % 2][:, 0:P],
                                    accq[:, 2048 * fc + P * ti:
                                         2048 * fc + P * (ti + 1)],
                                    sb_ident[:]))

    return nc



def _host_prep(inputs):
    x = np.asarray(inputs["L0_attn_hidden_states"])
    Wq = np.asarray(inputs["L0_attn_Wq"])
    bq = np.asarray(inputs["L0_attn_bq"])
    Wk = np.asarray(inputs["L0_attn_Wk"])
    bk = np.asarray(inputs["L0_attn_bk"])
    Wv = np.asarray(inputs["L0_attn_Wv"])
    bv = np.asarray(inputs["L0_attn_bv"])
    Wo = np.asarray(inputs["L0_attn_Wo"])
    gW = np.asarray(inputs["L0_ffn_gate_W_packed"])
    gS = np.asarray(inputs["L0_ffn_gate_scales"])
    uW = np.asarray(inputs["L0_ffn_up_W_packed"])
    uS = np.asarray(inputs["L0_ffn_up_scales"])
    dW = np.asarray(inputs["L0_ffn_down_W_packed"])
    dS = np.asarray(inputs["L0_ffn_down_scales"])

    pi, pin = PI_DM, PI_MLP

    wqT_ = np.ascontiguousarray(Wq.T[pi, :])
    wkT_ = np.ascontiguousarray(Wk.T[pi, :])
    wvT_ = np.ascontiguousarray(Wv.T[pi, :])
    woT_ = np.ascontiguousarray(Wo.T[:, pi])
    bq_cols = np.ascontiguousarray(bq.reshape(NH, P).T)
    bk_rep = np.ascontiguousarray(np.broadcast_to(bk, (P, KV)))
    bv_rep = np.ascontiguousarray(np.broadcast_to(bv, (P, KV)))

    def pack_bytes(Wp):
        return np.ascontiguousarray(Wp.view(np.uint8)[:, ::4])

    bt_g = np.ascontiguousarray(pack_bytes(gW).T[:, pin])
    bt_u = np.ascontiguousarray(pack_bytes(uW).T[:, pin])
    bt_d = np.ascontiguousarray(pack_bytes(dW).T[:, pi])

    def rep_scales(S):
        return np.repeat(np.ascontiguousarray(S.T), 16, axis=0)

    sc_g_ = np.ascontiguousarray(rep_scales(gS)[:, pin]).astype(np.float32)
    sc_u_ = np.ascontiguousarray(rep_scales(uS)[:, pin]).astype(np.float32)
    sc_d_ = np.ascontiguousarray(rep_scales(dS)[:, pi]).astype(np.float32)

    ident = np.eye(P, dtype=np.float32)
    import ml_dtypes
    eh = np.zeros((16, NH * P), dtype=ml_dtypes.bfloat16)
    for h in range(NH):
        eh[h, P * h:P * (h + 1)] = 1.0

    shared = dict(wqT=wqT_, wkT=wkT_, wvT=wvT_, woT=woT_, bq_cols=bq_cols,
                  bk_rep=bk_rep, bv_rep=bv_rep, bt_g=bt_g, bt_u=bt_u,
                  bt_d=bt_d, sc_g=sc_g_, sc_u=sc_u_, sc_d=sc_d_,
                  ident_f32=ident, eh_onehot=eh)

    in_maps = []
    for c in range(NCORE):
        b, h = c // 2, c % 2
        own = np.ascontiguousarray(x[b, 2048 * h:2048 * (h + 1), :][:, pi])
        oth = np.ascontiguousarray(x[b, 2048 * (1 - h):2048 * (2 - h), :][:, pi])
        m = dict(shared)
        m["x_full"] = np.ascontiguousarray(
            np.concatenate([own, oth], axis=0)).astype(np.float32, copy=False)
        in_maps.append(m)
    return in_maps


def kernel(**inputs):
    in_maps = _host_prep(inputs)
    nc = build_program()
    res = run_bass_kernel_spmd(nc, in_maps, core_ids=list(range(NCORE)))
    outs = [np.asarray(r["out"]) for r in res.results]
    inv = np.empty_like(PI_DM)
    inv[PI_DM] = np.arange(DM)
    full = np.empty((4, SEQ, DM), dtype=np.float32)
    for c in range(NCORE):
        b, h = c // 2, c % 2
        full[b, 2048 * h:2048 * (h + 1), :] = outs[c][:, inv]
    return full


# version 03
# speedup vs baseline: 1.0306x; 1.0306x over previous
import contextlib
import numpy as np
import concourse.bass as bass
import concourse.mybir as mybir
from concourse.bass_utils import run_bass_kernel_spmd

F32 = mybir.dt.float32
BF16 = mybir.dt.bfloat16
U8 = mybir.dt.uint8
ALU = mybir.AluOpType
ACT = mybir.ActivationFunctionType

P = 128
DM = 2048
SEQ = 4096
TOK = 2048
MLP = 16384
KV = 512
NH, KVH = 16, 4
NCORE = 8
EPS_LN = 1e-5
EPS_ATT = 1e-6

PHASE_LIMIT = 6
DEV_SKIP = set()


def _perm(dim):
    s = np.arange(dim)
    c2, p = s // 128, s % 128
    return (256 * (c2 // 2) + 2 * p + (c2 % 2)).astype(np.int64)


PI_DM = _perm(DM)
PI_MLP = _perm(MLP)


class Sem:
    def __init__(self, stack, nc, name):
        self.h = stack.enter_context(nc.semaphore(name))
        self.v = 0
        self.name = name

    def inc(self, ins, k=1):
        self.v += k
        return ins.then_inc(self.h, k)

    def dinc(self, ins):
        return self.inc(ins, 16)


def build_program():
    nc = bass.Bass("TRN2", target_bir_lowering=False, debug=False,
                   detect_race_conditions=False)

    def din(name, shape, dt):
        return nc.dram_tensor(name, list(shape), dt, kind="ExternalInput").ap()

    x_full = din("x_full", [SEQ, DM], F32)
    wqT = din("wqT", [DM, DM], F32)
    wkT = din("wkT", [DM, KV], F32)
    wvT = din("wvT", [DM, KV], F32)
    woT = din("woT", [DM, DM], F32)
    bq_cols = din("bq_cols", [P, NH], F32)
    bk_rep = din("bk_rep", [P, KV], F32)
    bv_rep = din("bv_rep", [P, KV], F32)
    bt_g = din("bt_g", [1024, MLP], U8)
    bt_u = din("bt_u", [1024, MLP], U8)
    bt_d = din("bt_d", [8192, DM], U8)
    sc_g = din("sc_g", [1024, MLP], F32)
    sc_u = din("sc_u", [1024, MLP], F32)
    sc_d = din("sc_d", [8192, DM], F32)
    ident = din("ident_f32", [P, P], F32)
    eh_in = din("eh_onehot", [16, NH * P], BF16)

    out = nc.dram_tensor("out", [TOK, DM], F32, kind="ExternalOutput").ap()

    xn_s = nc.dram_tensor("xn_s", [SEQ, DM], BF16, kind="Internal").ap()
    xn2_s = nc.dram_tensor("xn2_s", [TOK, DM], BF16, kind="Internal").ap()
    wq_s = nc.dram_tensor("wq_s", [DM, DM], BF16, kind="Internal").ap()
    wo_s = nc.dram_tensor("wo_s", [DM, DM], BF16, kind="Internal").ap()
    inter_s = nc.dram_tensor("inter_s", [MLP, TOK], BF16, kind="Internal").ap()

    st = contextlib.ExitStack()
    with st:
        SS = lambda name: Sem(st, nc, name)
        s_ld = SS("s_ld"); s_cast = SS("s_cast"); s_cst = SS("s_cst")
        s_p0 = SS("s_p0")
        s_x1 = SS("s_x1"); s_st1 = SS("s_st1"); s_var1 = SS("s_var1")
        s_sd1 = SS("s_sd1"); s_fv1 = SS("s_fv1"); s_nm1 = SS("s_nm1")
        s_xn = SS("s_xn")
        s_xnt2 = SS("s_xnt2"); s_kmm = SS("s_kmm"); s_eluq = SS("s_eluq")
        s_exp2 = SS("s_exp2"); s_elu1 = SS("s_elu1"); s_skrdy = SS("s_skrdy")
        s_stat = SS("s_stat"); s_fv2 = SS("s_fv2"); s_p2 = SS("s_p2")
        s_x3 = SS("s_x3"); s_wq = SS("s_wq"); s_wo = SS("s_wo")
        s_xnt3 = SS("s_xnt3"); s_qmm = SS("s_qmm"); s_qt2 = SS("s_qt2")
        s_qexp = SS("s_qexp"); s_qps = SS("s_qps"); s_sqrdy = SS("s_sqrdy")
        s_nrmm = SS("s_nrmm"); s_fv3 = SS("s_fv3"); s_recip = SS("s_recip")
        s_bc = SS("s_bc"); s_sqn = SS("s_sqn"); s_qkv = SS("s_qkv")
        s_attev = SS("s_attev"); s_omm = SS("s_omm"); s_yev = SS("s_yev")
        s_yst = SS("s_yst")
        s_st2 = SS("s_st2"); s_var2 = SS("s_var2"); s_sd2 = SS("s_sd2")
        s_fv4 = SS("s_fv4"); s_nm2 = SS("s_nm2"); s_xn2 = SS("s_xn2")
        s_ph = SS("s_ph")
        s_byt = SS("s_byt"); s_scl = SS("s_scl")
        s_d1 = SS("s_d1"); s_d2 = SS("s_d2"); s_d3 = SS("s_d3"); s_d4 = SS("s_d4")
        s_x2t = SS("s_x2t"); s_gmm = SS("s_gmm"); s_gel = SS("s_gel")
        s_upc = SS("s_upc"); s_int = SS("s_int"); s_ist = SS("s_ist")
        s_ild = SS("s_ild"); s_dmm = SS("s_dmm"); s_acc = SS("s_acc")
        s_yld = SS("s_yld"); s_tp = SS("s_tp"); s_fin = SS("s_fin")
        s_ost = SS("s_ost"); s_wfree = SS("s_wfree")

        sb_eps = st.enter_context(nc.sbuf_tensor("c_eps", [P, 1], F32))
        sb_cs2 = st.enter_context(nc.sbuf_tensor("c_cs2", [P, 1], F32))
        sb_cb4 = st.enter_context(nc.sbuf_tensor("c_cb4", [P, 1], F32))
        sb_cb8 = st.enter_context(nc.sbuf_tensor("c_cb8", [P, 1], F32))
        sb_half = st.enter_context(nc.sbuf_tensor("c_half", [P, 1], F32))
        cs4 = st.enter_context(nc.sbuf_tensor("c_cs4", [P, 1], F32))
        cb16 = st.enter_context(nc.sbuf_tensor("c_cb16", [P, 1], F32))
        sb_ident = st.enter_context(nc.sbuf_tensor("c_ident", [P, P], F32))

        pb = [st.enter_context(nc.psum_tensor(f"pb{i}", [P, 512], F32))
              for i in range(8)]

        phase_round = [0]

        def barrier(block):
            phase_round[0] += 1
            target = 5 * phase_round[0]
            for ename in ("sync", "scalar", "vector", "tensor", "gpsimd"):
                def f(eng):
                    s_ph.inc(eng.nop(), 1)
                    eng.wait_ge(s_ph.h, target)
                getattr(block, ename)(f)
            assert s_ph.v == target

        stA = st.enter_context(contextlib.ExitStack())

        def sbA(name, shape, dt):
            return stA.enter_context(nc.sbuf_tensor(name, list(shape), dt))

        xbuf = sbA("xbuf", [P, 4 * DM], F32)
        ybuf = sbA("ybuf", [P, 4 * DM], F32)
        lnsc = sbA("lnsc", [P, DM], F32)
        xnb = sbA("xnb", [P, 4 * DM], BF16)
        l_sum = sbA("l_sum", [P, 1], F32)
        l_ssq = sbA("l_ssq", [P, 1], F32)
        l_mean = sbA("l_mean", [P, 1], F32)
        l_msq = sbA("l_msq", [P, 1], F32)
        l_m2t = sbA("l_m2t", [P, 1], F32)
        l_var = sbA("l_var", [P, 1], F32)
        l_sd = sbA("l_sd", [P, 1], F32)
        l_rstd = sbA("l_rstd", [P, 1], F32)
        m_sum = sbA("m_sum", [P, 1], F32)
        m_ssq = sbA("m_ssq", [P, 1], F32)
        m_mean = sbA("m_mean", [P, 1], F32)
        m_msq = sbA("m_msq", [P, 1], F32)
        m_m2t = sbA("m_m2t", [P, 1], F32)
        m_var = sbA("m_var", [P, 1], F32)
        m_sd = sbA("m_sd", [P, 1], F32)
        m_rstd = sbA("m_rstd", [P, 1], F32)
        wkb = sbA("wkb", [P, 16 * KV], BF16)
        wvb = sbA("wvb", [P, 16 * KV], BF16)
        bkr = sbA("bkr", [P, KV], F32)
        bvr = sbA("bvr", [P, KV], F32)
        bqc = sbA("bqc", [P, NH], F32)
        xnt0 = sbA("xnt0", [P, 16 * 512], BF16)
        skb = [sbA(f"skb{i}", [P, KV], BF16) for i in range(2)]
        svb = [sbA(f"svb{i}", [P, 528], BF16) for i in range(2)]
        e_t1 = sbA("e_t1", [P, KV], F32)
        e_t2 = sbA("e_t2", [P, KV], F32)
        e_e = sbA("e_e", [P, KV], F32)
        e_r = sbA("e_r", [P, KV], F32)
        ones1 = sbA("ones1", [P, 1], BF16)
        kvb = sbA("kvb", [P, KVH * P], BF16)
        ksumf = sbA("ksumf", [P, KVH], F32)
        KS = sbA("KS", [P, NH * 16], BF16)
        Eh = sbA("Eh", [16, NH * P], BF16)
        wofs = [wkb, wvb]
        q_t2, q_e, q_r = e_t2, e_e, e_r
        xn2b = xnb
        stP0 = contextlib.ExitStack()
        cw = [stP0.enter_context(nc.sbuf_tensor(f"cw{i}", [P, DM], F32))
              for i in range(2)]
        cwb = [stP0.enter_context(nc.sbuf_tensor(f"cwb{i}", [P, DM], BF16))
               for i in range(2)]

        with nc.Block("p0") as block:
            @block.sync
            def _(sync):
                s_ld.dinc(sync.dma_start(out=bkr[:], in_=bk_rep[:]))
                s_ld.dinc(sync.dma_start(out=bvr[:], in_=bv_rep[:]))
                s_ld.dinc(sync.dma_start(out=bqc[:], in_=bq_cols[:]))
                s_ld.dinc(sync.dma_start(out=sb_ident[:], in_=ident[:]))
                s_ld.dinc(sync.dma_start(out=Eh[:], in_=eh_in[:]))
                s_ld.dinc(sync.dma_start(
                    out=xbuf[:, 0:16 * KV].rearrange("p (c q) -> p c q", q=KV),
                    in_=wkT[:].rearrange("(c p) q -> p c q", p=P)))
                s_ld.dinc(sync.dma_start(
                    out=ybuf[:, 0:16 * KV].rearrange("p (c q) -> p c q", q=KV),
                    in_=wvT[:].rearrange("(c p) q -> p c q", p=P)))
                for i in range(32):
                    src = wqT if i < 16 else woT
                    c = i % 16
                    if i >= 2:
                        sync.wait_ge(s_cast.h, i - 1)
                    s_ld.dinc(sync.dma_start(out=cw[i % 2][:],
                                             in_=src[P * c:P * (c + 1), :]))
                    if i >= 1:
                        j = i - 1
                        dstj = wq_s if j < 16 else wo_s
                        cj = j % 16
                        sync.wait_ge(s_cast.h, j + 1)
                        s_cst.dinc(sync.dma_start(
                            out=dstj[P * cj:P * (cj + 1), :], in_=cwb[j % 2][:]))
                sync.wait_ge(s_cast.h, 32)
                s_cst.dinc(sync.dma_start(out=wo_s[P * 15:P * 16, :],
                                          in_=cwb[31 % 2][:]))
                sync.wait_ge(s_cst.h, s_cst.v)

            @block.vector
            def _(v):
                v.memset(sb_eps[:], EPS_LN)
                v.memset(sb_cs2[:], 2.0)
                v.memset(sb_cb4[:], -4.0)
                v.memset(sb_cb8[:], -8.0)
                v.memset(sb_half[:], 0.5)
                v.memset(cs4[:], 4.0)
                v.memset(cb16[:], -16.0)
                v.memset(ones1[:], 1.0)
                v.memset(KS[:], 0.0)
                for i in range(2):
                    for kvh in range(KVH):
                        v.memset(svb[i][:, 132 * kvh + 128:132 * kvh + 129], 1.0)
                v.wait_ge(s_ld.h, 7 * 16)
                v.tensor_copy(wkb[:], xbuf[:, 0:16 * KV])
                s_p0.inc(v.tensor_copy(wvb[:], ybuf[:, 0:16 * KV]))
                for i in range(32):
                    v.wait_ge(s_ld.h, (8 + i) * 16)
                    if i >= 2:
                        v.wait_ge(s_cst.h, 16 * (i - 1))
                    s_cast.inc(v.tensor_copy(cwb[i % 2][:], cw[i % 2][:]))

        stP0.close()
        wqcol = [sbA(f"wqcol{i}", [P, 16 * P], BF16) for i in range(2)]
        sqb = sbA("sqb", [P, NH * 512], BF16)
        sqnb = [sbA(f"sqnb{i}", [P, 512], BF16) for i in range(2)]
        nrm_f = sbA("nrm_f", [16, 512], F32)
        nrm_r = sbA("nrm_r", [16, 512], F32)
        nrm_b = sbA("nrm_b", [16, 512], BF16)
        attnT = sbA("attnT", [P, NH * 512], BF16)

        NT1 = 8
        with nc.Block("p1") as block:
            @block.sync
            def _(sync):
                for t in range(NT1):
                    if t >= 2:
                        sync.wait_ge(s_nm1.h, 4 * (t - 1))
                    s_x1.dinc(sync.dma_start(
                        out=(xbuf if t % 2 == 0 else ybuf)[:]
                            .rearrange("p (s q) -> p s q", q=DM),
                        in_=x_full[512 * t:512 * (t + 1), :]
                            .rearrange("(s p) q -> p s q", p=P)))
                    if t >= 1:
                        j = t - 1
                        sync.wait_ge(s_nm1.h, 4 * (j + 1))
                        s_xn.dinc(sync.dma_start(
                            out=xn_s[512 * j:512 * (j + 1), :]
                                .rearrange("(s p) q -> p s q", p=P),
                            in_=xnb[:].rearrange("p (s q) -> p s q", q=DM)))
                sync.wait_ge(s_nm1.h, 4 * NT1)
                s_xn.dinc(sync.dma_start(
                    out=xn_s[512 * (NT1 - 1):512 * NT1, :]
                        .rearrange("(s p) q -> p s q", p=P),
                    in_=xnb[:].rearrange("p (s q) -> p s q", q=DM)))
                sync.wait_ge(s_xn.h, s_xn.v)

            @block.scalar
            def _(sc):
                for t in range(NT1):
                    buf = xbuf if t % 2 == 0 else ybuf
                    for s in range(4):
                        i = 4 * t + s
                        sc.wait_ge(s_x1.h, 16 * (t + 1))
                        xs = buf[:, DM * s:DM * (s + 1)]
                        sc.activation(out=lnsc[:], in_=xs, func=ACT.Square,
                                      accum_out=l_ssq[:])
                        s_st1.inc(sc.activation(out=lnsc[:], in_=xs, func=ACT.Copy,
                                                accum_out=l_sum[:]))
                        sc.wait_ge(s_var1.h, i + 1)
                        s_sd1.inc(sc.activation(out=l_sd[:], in_=l_var[:],
                                                func=ACT.Sqrt, bias=sb_eps[:]))

            @block.vector
            def _(v):
                for t in range(NT1):
                    buf = xbuf if t % 2 == 0 else ybuf
                    for s in range(4):
                        i = 4 * t + s
                        v.wait_ge(s_st1.h, i + 1)
                        if s == 0 and t >= 1:
                            v.wait_ge(s_xn.h, 16 * t)
                        v.tensor_scalar(l_mean[:], l_sum[:], 1.0 / DM, None, ALU.mult)
                        s_fv1.inc(v.tensor_scalar(l_msq[:], l_ssq[:], 1.0 / DM,
                                                  None, ALU.mult))
                        v.wait_ge(s_fv1.h, 3 * i + 1)
                        s_fv1.inc(v.tensor_mul(l_m2t[:], l_mean[:], l_mean[:]))
                        v.wait_ge(s_fv1.h, 3 * i + 2)
                        s_var1.inc(v.tensor_sub(l_var[:], l_msq[:], l_m2t[:]))
                        v.wait_ge(s_sd1.h, i + 1)
                        s_fv1.inc(v.reciprocal(l_rstd[:], l_sd[:]))
                        v.wait_ge(s_fv1.h, 3 * i + 3)
                        xs = buf[:, DM * s:DM * (s + 1)]
                        s_nm1.inc(v.tensor_scalar(
                            xnb[:, DM * s:DM * (s + 1)], xs, l_mean[:], l_rstd[:],
                            ALU.subtract, ALU.mult))

        assert s_nm1.v == 32 and s_st1.v == 32 and s_fv1.v == 96

        if PHASE_LIMIT >= 2:
            with nc.Block("p2") as block:
                @block.scalar
                def _(sc):
                    for t in range(NT1):
                        sc.wait_ge(s_xn.h, 16 * (t + 1))
                        if t >= 1:
                            sc.wait_ge(s_stat.h, 4 * t)
                        for fc in range(16):
                            s_xnt2.dinc(sc.dma_start(
                                out=xnt0[:, 512 * fc:512 * (fc + 1)],
                                in_=xn_s[512 * t:512 * (t + 1), P * fc:P * (fc + 1)],
                                transpose=True))
                        for s in range(4):
                            gs = 4 * t + s
                            sc.wait_ge(s_eluq.h, gs + 1)
                            if gs >= 1:
                                sc.wait_ge(s_skrdy.h, gs)
                            s_exp2.inc(sc.activation(out=e_e[:], in_=e_t2[:],
                                                     func=ACT.Exp))

                @block.tensor
                def _(tn):
                    tn.wait_ge(s_p0.h, 1)
                    for t in range(NT1):
                        tn.wait_ge(s_xnt2.h, 256 * (t + 1))
                        for s in range(4):
                            gs = 4 * t + s
                            par = gs % 2
                            if gs >= 2:
                                tn.wait_ge(s_elu1.h, gs - 1)
                            for fc in range(16):
                                tn.matmul(pb[0 + par][:],
                                          xnt0[:, 512 * fc + P * s:512 * fc + P * (s + 1)],
                                          wkb[:, 512 * fc:512 * (fc + 1)],
                                          start=(fc == 0), stop=(fc == 15))
                            for fc in range(16):
                                ins = tn.matmul(pb[2 + par][:],
                                                xnt0[:, 512 * fc + P * s:512 * fc + P * (s + 1)],
                                                wvb[:, 512 * fc:512 * (fc + 1)],
                                                start=(fc == 0), stop=(fc == 15))
                            s_kmm.inc(ins)
                            tn.wait_ge(s_elu1.h, gs + 1)
                            tn.wait_ge(s_skrdy.h, gs + 1)
                            for kvh in range(KVH):
                                ins = tn.matmul(pb[4 + kvh][:, 0:P + 1],
                                                skb[par][:, P * kvh:P * (kvh + 1)],
                                                svb[par][:, 132 * kvh:132 * kvh + P + 1],
                                                start=(gs == 0), stop=(gs == 31))
                            s_stat.inc(ins)

                @block.vector
                def _(v):
                    for t in range(NT1):
                        for s in range(4):
                            gs = 4 * t + s
                            par = gs % 2
                            v.wait_ge(s_kmm.h, gs + 1)
                            if gs >= 2:
                                v.wait_ge(s_stat.h, gs - 1)
                            v.tensor_add(e_t1[:], pb[0 + par][:], bkr[:])
                            s_eluq.inc(v.tensor_scalar(e_t2[:], e_t1[:], 0.0,
                                                       None, ALU.min))
                            v.tensor_scalar(e_r[:], e_t1[:], 0.0, None, ALU.max)
                            s_elu1.inc(v.tensor_add(
                                svb[par][:, 0:528].rearrange(
                                    "p (k d) -> p k d", d=132)[:, :, 0:P],
                                pb[2 + par][:].rearrange("p (k d) -> p k d", d=P),
                                bvr[:].rearrange("p (k d) -> p k d", d=P)))
                            v.wait_ge(s_exp2.h, gs + 1)
                            s_skrdy.inc(v.tensor_add(skb[par][:], e_e[:], e_r[:]))
                    v.wait_ge(s_stat.h, 32)
                    for kvh in range(KVH):
                        v.tensor_copy(kvb[:, P * kvh:P * (kvh + 1)],
                                      pb[4 + kvh][:, 0:P])
                        ins = v.tensor_copy(ksumf[:, kvh:kvh + 1],
                                            pb[4 + kvh][:, P:P + 1])
                    s_fv2.inc(ins)
                    v.wait_ge(s_fv2.h, 1)
                    for h in range(NH):
                        ins = v.tensor_copy(KS[:, 16 * h + h:16 * h + h + 1],
                                            ksumf[:, h // 4:h // 4 + 1])
                    s_p2.inc(ins)
            assert s_kmm.v == 32 and s_stat.v == 32 and s_skrdy.v == 32

        if PHASE_LIMIT >= 3:
            NT3 = 4
            with nc.Block("p3") as block:
                @block.sync
                def _(sync):
                    sync.wait_ge(s_nm1.h, 32)
                    if PHASE_LIMIT >= 2:
                        sync.wait_ge(s_stat.h, 32)
                    for t in range(NT3):
                        if t >= 1:
                            sync.wait_ge(s_yev.h, 16 * t)
                        s_x3.dinc(sync.dma_start(
                            out=xbuf[:].rearrange("p (s q) -> p s q", q=DM),
                            in_=x_full[512 * t:512 * (t + 1), :]
                                .rearrange("(s p) q -> p s q", p=P)))
                        if t >= 1:
                            j = t - 1
                            sync.wait_ge(s_yev.h, 16 * (j + 1))
                            s_yst.dinc(sync.dma_start(
                                out=out[512 * j:512 * (j + 1), :]
                                    .rearrange("(s p) q -> p s q", p=P),
                                in_=ybuf[:].rearrange("p (s q) -> p s q", q=DM)))
                            sync.wait_ge(s_nm2.h, 4 * (j + 1))
                            s_xn2.dinc(sync.dma_start(
                                out=xn2_s[512 * j:512 * (j + 1), :]
                                    .rearrange("(s p) q -> p s q", p=P),
                                in_=xn2b[:].rearrange("p (s q) -> p s q", q=DM)))
                        for h in range(NH):
                            i = t * NH + h
                            if i >= 2:
                                sync.wait_ge(s_qmm.h, i - 1)
                            s_wq.dinc(sync.dma_start(
                                out=wqcol[i % 2][:].rearrange("p (c r) -> p c r", r=P),
                                in_=wq_s[:, P * h:P * (h + 1)]
                                    .rearrange("(c p) r -> p c r", p=P)))
                        for fs in range(4):
                            i = t * 4 + fs
                            if i >= 2:
                                sync.wait_ge(s_omm.h, 4 * (i - 1))
                            s_wo.dinc(sync.dma_start(
                                out=wofs[i % 2][:].rearrange("p (c r) -> p c r", r=512),
                                in_=wo_s[:, 512 * fs:512 * (fs + 1)]
                                    .rearrange("(c p) r -> p c r", p=P)))
                    j = NT3 - 1
                    sync.wait_ge(s_yev.h, 16 * (j + 1))
                    s_yst.dinc(sync.dma_start(
                        out=out[512 * j:512 * (j + 1), :]
                            .rearrange("(s p) q -> p s q", p=P),
                        in_=ybuf[:].rearrange("p (s q) -> p s q", q=DM)))
                    sync.wait_ge(s_nm2.h, 4 * (j + 1))
                    s_xn2.dinc(sync.dma_start(
                        out=xn2_s[512 * j:512 * (j + 1), :]
                            .rearrange("(s p) q -> p s q", p=P),
                        in_=xn2b[:].rearrange("p (s q) -> p s q", q=DM)))
                    sync.wait_ge(s_yst.h, s_yst.v)
                    sync.wait_ge(s_xn2.h, s_xn2.v)

                @block.scalar
                def _(sc):
                    for t in range(NT3):
                        if t == 0:
                            sc.wait_ge(s_xn.h, 16 * NT1)
                        if t == 0 and PHASE_LIMIT >= 2:
                            sc.wait_ge(s_stat.h, 32)
                        if t >= 1:
                            sc.wait_ge(s_nrmm.h, t)
                        for fc in range(16):
                            s_xnt3.dinc(sc.dma_start(
                                out=xnt0[:, 512 * fc:512 * (fc + 1)],
                                in_=xn_s[512 * t:512 * (t + 1), P * fc:P * (fc + 1)],
                                transpose=True))
                        for h in range(NH):
                            gh = t * NH + h
                            sc.wait_ge(s_qt2.h, gh + 1)
                            if gh >= 1:
                                sc.wait_ge(s_sqrdy.h, gh)
                            s_qexp.inc(sc.activation(out=q_e[:], in_=q_t2[:],
                                                     func=ACT.Exp))
                        for s in range(4):
                            i = 4 * t + s
                            sc.wait_ge(s_yev.h, 16 * (t + 1))
                            ys = ybuf[:, DM * s:DM * (s + 1)]
                            sc.activation(out=lnsc[:], in_=ys, func=ACT.Square,
                                          accum_out=m_ssq[:])
                            s_st2.inc(sc.activation(out=lnsc[:], in_=ys, func=ACT.Copy,
                                                    accum_out=m_sum[:]))
                            sc.wait_ge(s_var2.h, i + 1)
                            s_sd2.inc(sc.activation(out=m_sd[:], in_=m_var[:],
                                                    func=ACT.Sqrt, bias=sb_eps[:]))

                @block.tensor
                def _(tn):
                    for t in range(NT3):
                        tn.wait_ge(s_xnt3.h, 256 * (t + 1))
                        if PHASE_LIMIT >= 2:
                            tn.wait_ge(s_p2.h, 1)
                        if t >= 1:
                            tn.wait_ge(s_recip.h, t)
                            tn.wait_ge(s_yev.h, 16 * t)
                        for h in range(NH):
                            gh = t * NH + h
                            tn.wait_ge(s_wq.h, 16 * (gh + 1))
                            if gh >= 2:
                                tn.wait_ge(s_qps.h, gh - 1)
                            for fc in range(16):
                                ins = tn.matmul(pb[0 + h % 2][:],
                                                wqcol[gh % 2][:, P * fc:P * (fc + 1)],
                                                xnt0[:, 512 * fc:512 * (fc + 1)],
                                                start=(fc == 0), stop=(fc == 15))
                            s_qmm.inc(ins)
                            tn.wait_ge(s_sqrdy.h, gh + 1)
                            ins = tn.matmul(pb[4][:16, :],
                                            KS[:, 16 * h:16 * (h + 1)],
                                            sqb[:, 512 * h:512 * (h + 1)],
                                            start=(h == 0), stop=(h == 15))
                        s_nrmm.inc(ins)
                        tn.wait_ge(s_recip.h, t + 1)
                        for h in range(NH):
                            gh = t * NH + h
                            if gh >= 2:
                                tn.wait_ge(s_sqn.h, gh - 1)
                            s_bc.inc(tn.matmul(pb[2 + h % 2][:],
                                               Eh[:, P * h:P * (h + 1)],
                                               nrm_b[:], start=True, stop=True))
                            if gh >= 2:
                                tn.wait_ge(s_attev.h, gh - 1)
                            tn.wait_ge(s_sqn.h, gh + 1)
                            kvh = h // 4
                            s_qkv.inc(tn.matmul(pb[6 + h % 2][:],
                                                kvb[:, P * kvh:P * (kvh + 1)],
                                                sqnb[h % 2][:], start=True, stop=True))
                        tn.wait_ge(s_attev.h, NH * (t + 1))
                        for fs in range(4):
                            gf = t * 4 + fs
                            tn.wait_ge(s_wo.h, 16 * (gf + 1))
                            for tsub in range(4):
                                idx = t * 16 + fs * 4 + tsub
                                if idx >= 2:
                                    tn.wait_ge(s_yev.h, idx - 1)
                                for h in range(NH):
                                    ins = tn.matmul(pb[4 + idx % 2][:],
                                                    attnT[:, 512 * h + P * tsub:
                                                          512 * h + P * (tsub + 1)],
                                                    wofs[gf % 2][:, 512 * h:512 * (h + 1)],
                                                    start=(h == 0), stop=(h == 15))
                                s_omm.inc(ins)

                @block.vector
                def _(v):
                    for t in range(NT3):
                        if t >= 1:
                            v.wait_ge(s_nrmm.h, t)
                            v.wait_ge(s_omm.h, 16 * t)
                        for h in range(NH):
                            gh = t * NH + h
                            v.wait_ge(s_qmm.h, gh + 1)
                            bq_ap = bqc[:, h:h + 1]
                            s_qt2.inc(v.tensor_scalar(q_t2[:], pb[0 + h % 2][:],
                                                      bq_ap, 0.0, ALU.add, ALU.min))
                            s_qps.inc(v.tensor_scalar(q_r[:], pb[0 + h % 2][:],
                                                      bq_ap, 0.0, ALU.add, ALU.max))
                            v.wait_ge(s_qexp.h, gh + 1)
                            s_sqrdy.inc(v.tensor_add(sqb[:, 512 * h:512 * (h + 1)],
                                                     q_e[:], q_r[:]))
                        v.wait_ge(s_nrmm.h, t + 1)
                        s_fv3.inc(v.tensor_scalar(nrm_f[:], pb[4][:16, :], EPS_ATT,
                                                  None, ALU.add))
                        v.wait_ge(s_fv3.h, 3 * t + 1)
                        s_fv3.inc(v.reciprocal(nrm_r[:], nrm_f[:]))
                        v.wait_ge(s_fv3.h, 3 * t + 2)
                        s_fv3.inc(v.tensor_copy(nrm_b[:], nrm_r[:]))
                        v.wait_ge(s_fv3.h, 3 * t + 3)
                        s_recip.inc(v.nop())
                        for h in range(NH):
                            gh = t * NH + h
                            v.wait_ge(s_bc.h, gh + 1)
                            if gh >= 2:
                                v.wait_ge(s_qkv.h, gh - 1)
                            s_sqn.inc(v.tensor_mul(sqnb[h % 2][:],
                                                   sqb[:, 512 * h:512 * (h + 1)],
                                                   pb[2 + h % 2][:]))
                            v.wait_ge(s_qkv.h, gh + 1)
                            s_attev.inc(v.tensor_copy(
                                attnT[:, 512 * h:512 * (h + 1)], pb[6 + h % 2][:]))
                        if t >= 1:
                            v.wait_ge(s_yst.h, 16 * t)
                            v.wait_ge(s_nm2.h, 4 * t)
                        v.wait_ge(s_x3.h, 16 * (t + 1))
                        for fs in range(4):
                            for tsub in range(4):
                                idx = t * 16 + fs * 4 + tsub
                                v.wait_ge(s_omm.h, idx + 1)
                                s_yev.inc(v.tensor_add(
                                    ybuf[:, DM * tsub + 512 * fs:
                                         DM * tsub + 512 * (fs + 1)],
                                    pb[4 + idx % 2][:],
                                    xbuf[:, DM * tsub + 512 * fs:
                                         DM * tsub + 512 * (fs + 1)]))
                        for s in range(4):
                            i = 4 * t + s
                            v.wait_ge(s_st2.h, i + 1)
                            if s == 0 and t >= 1:
                                v.wait_ge(s_xn2.h, 16 * t)
                            v.tensor_scalar(m_mean[:], m_sum[:], 1.0 / DM, None,
                                            ALU.mult)
                            s_fv4.inc(v.tensor_scalar(m_msq[:], m_ssq[:], 1.0 / DM,
                                                      None, ALU.mult))
                            v.wait_ge(s_fv4.h, 3 * i + 1)
                            s_fv4.inc(v.tensor_mul(m_m2t[:], m_mean[:], m_mean[:]))
                            v.wait_ge(s_fv4.h, 3 * i + 2)
                            s_var2.inc(v.tensor_sub(m_var[:], m_msq[:], m_m2t[:]))
                            v.wait_ge(s_sd2.h, i + 1)
                            s_fv4.inc(v.reciprocal(m_rstd[:], m_sd[:]))
                            v.wait_ge(s_fv4.h, 3 * i + 3)
                            ys = ybuf[:, DM * s:DM * (s + 1)]
                            s_nm2.inc(v.tensor_scalar(
                                xn2b[:, DM * s:DM * (s + 1)], ys, m_mean[:],
                                m_rstd[:], ALU.subtract, ALU.mult))

            assert s_qmm.v == 64 and s_omm.v == 64 and s_yev.v == 64

        if PHASE_LIMIT >= 4:
            stA.close()
            stB = st.enter_context(contextlib.ExitStack())

            def sbB(name, shape, dt):
                return stB.enter_context(nc.sbuf_tensor(name, list(shape), dt))

            DP = 4
            bb = [sbB(f"bb{i}", [P, 512], U8) for i in range(DP)]
            cat = [sbB("cat0", [P, 2048], U8)]
            cf = [sbB(f"cf{i}", [P, 2048], BF16) for i in range(DP)]
            sgm = [sbB(f"sgm{i}", [P, 1024], BF16) for i in range(DP)]
            u1b = [sbB(f"u1b{i}", [P, 1024], BF16) for i in range(DP)]
            u2b = [sbB(f"u2b{i}", [P, 1024], BF16) for i in range(DP)]
            Sf = [sbB(f"Sf{i}", [P, 512], F32) for i in range(2)]
            SbD = [sbB(f"SbD{i}", [P, 1024], BF16) for i in range(DP)]
            m2b = [sbB("m2b0", [P, 1024], BF16)]
            sSb = [sbB("sSb0", [P, 1024], BF16)]
            m3b = [sbB(f"m3b{i}", [P, 1024], BF16) for i in range(DP)]
            wbuf = [[sbB(f"wbuf{m}{i}", [P, 16 * 512], BF16) for i in range(2)]
                    for m in range(2)]

            DT = [0]

            def emit_decode_sync(sync, bsrc, ssrc, r0, c0):
                i = DT[0]
                if i >= DP:
                    sync.wait_ge(s_d1.h, i - DP + 1)
                if i >= 2:
                    sync.wait_ge(s_d2.h, i - 1)
                s_byt.dinc(sync.dma_start(out=bb[i % DP][:],
                                          in_=bsrc[r0:r0 + P, c0:c0 + 512]))
                s_scl.dinc(sync.dma_start(out=Sf[i % 2][:],
                                          in_=ssrc[r0:r0 + P, c0:c0 + 512]))

            def emit_decode_d1(v, i):
                par = i % DP
                v.wait_ge(s_byt.h, 16 * (i + 1))
                if i >= DP:
                    v.wait_ge(s_d2.h, i - DP + 1)
                    v.wait_ge(s_d4.h, i - DP + 1)
                v.tensor_scalar(cat[0][:, 0:512], bb[par][:], 7, None,
                                ALU.bitwise_and)
                v.tensor_scalar(cat[0][:, 512:1024], bb[par][:], 4, 7,
                                ALU.logical_shift_right, ALU.bitwise_and)
                v.tensor_scalar(cat[0][:, 1024:1536], bb[par][:], 8, None,
                                ALU.bitwise_and)
                v.tensor_scalar(cat[0][:, 1536:2048], bb[par][:], 4, 8,
                                ALU.logical_shift_right, ALU.bitwise_and)
                v.tensor_copy(cf[par][:], cat[0][:])
                s_d1.inc(v.tensor_scalar(sgm[par][:], cf[par][:, 1024:2048],
                                         -0.25, 1.0, ALU.mult, ALU.add))

            def emit_decode_d2(sc, i):
                par = i % DP
                sc.wait_ge(s_d1.h, i + 1)
                if i >= DP:
                    sc.wait_ge(s_d3.h, i - DP + 1)
                    sc.wait_ge(s_d4.h, i - DP + 1)
                sc.activation(out=u1b[par][:], in_=cf[par][:, 0:1024],
                              func=ACT.Identity, bias=sb_cb4[:], scale=sb_cs2[:])
                sc.activation(out=u2b[par][:], in_=cf[par][:, 0:1024],
                              func=ACT.Identity, bias=cb16[:], scale=cs4[:])
                sc.activation(out=SbD[par][:, 0:512], in_=Sf[i % 2][:],
                              func=ACT.Identity, scale=sb_half[:])
                s_d2.inc(sc.activation(out=SbD[par][:, 512:1024], in_=Sf[i % 2][:],
                                       func=ACT.Identity, scale=sb_half[:]))

            def emit_decode_d3(v, i):
                par = i % DP
                v.wait_ge(s_d2.h, i + 1)
                if i >= DP:
                    v.wait_ge(s_d4.h, i - DP + 1)
                v.tensor_max(m2b[0][:], u1b[par][:], u2b[par][:])
                s_d3.inc(v.tensor_max(m3b[par][:], m2b[0][:],
                                      cf[par][:, 0:1024]))

            def emit_decode_d4(g, i, wdst, wfree):
                par = i % DP
                g.wait_ge(s_d3.h, i + 1)
                if wfree is not None:
                    g.wait_ge(wfree[0].h, wfree[1])
                g.tensor_mul(sSb[0][:], sgm[par][:], SbD[par][:])
                s_d4.inc(g.tensor_mul(wdst, m3b[par][:], sSb[0][:]))

            stP4 = stB.enter_context(contextlib.ExitStack())
            x2t = stP4.enter_context(nc.sbuf_tensor("x2t", [P, 16 * 2048], BF16))
            intb = [stP4.enter_context(nc.sbuf_tensor(f"intb{i}", [P, 2048], BF16))
                    for i in range(2)]
            ggb = [stP4.enter_context(nc.sbuf_tensor(f"ggb{i}", [P, 512], BF16))
                   for i in range(2)]
            uub = [stP4.enter_context(nc.sbuf_tensor("uub0", [P, 512], BF16))]

            NG = 32

            def p4_tile_params(i):
                g, r = divmod(i, 16)
                m, c = divmod(r, 8)
                bsrc = bt_g if m == 0 else bt_u
                ssrc = sc_g if m == 0 else sc_u
                return g, m, c, bsrc, ssrc

            with nc.Block("p4") as block:
                @block.scalar
                def _(sc):
                    sc.wait_ge(s_xn2.h, 64)
                    for fc in range(16):
                        s_x2t.dinc(sc.dma_start(
                            out=x2t[:, 2048 * fc:2048 * (fc + 1)],
                            in_=xn2_s[:, P * fc:P * (fc + 1)], transpose=True))
                    for g in range(NG):
                        for j in range(16):
                            emit_decode_d2(sc, g * 16 + j)
                            if g >= 1:
                                eidx = (g - 1) * 16 + j
                                ts = j % 4
                                sc.wait_ge(s_gmm.h, eidx + 1)
                                if eidx >= 2:
                                    sc.wait_ge(s_int.h, eidx - 1)
                                s_gel.inc(sc.activation(
                                    out=ggb[eidx % 2][:], in_=pb[ts][:],
                                    func=ACT.Gelu_apprx_tanh))
                    for j in range(16):
                        eidx = (NG - 1) * 16 + j
                        ts = j % 4
                        sc.wait_ge(s_gmm.h, eidx + 1)
                        if eidx >= 2:
                            sc.wait_ge(s_int.h, eidx - 1)
                        s_gel.inc(sc.activation(
                            out=ggb[eidx % 2][:], in_=pb[ts][:],
                            func=ACT.Gelu_apprx_tanh))

                @block.sync
                def _(sync):
                    sync.wait_ge(s_xn2.h, 64)
                    for g in range(NG):
                        for j in range(16):
                            _, m, c, bsrc, ssrc = p4_tile_params(g * 16 + j)
                            emit_decode_sync(sync, bsrc, ssrc, 128 * c, 512 * g)
                            DT[0] += 1
                            if g >= 1 and j % 4 == 3:
                                ncg = j // 4
                                ncc = (g - 1) * 4 + ncg
                                sync.wait_ge(s_int.h, 4 * (ncc + 1))
                                s_ist.dinc(sync.dma_start(
                                    out=inter_s[P * ncc:P * (ncc + 1), :],
                                    in_=intb[ncg % 2][:]))
                    for ncg in range(4):
                        ncc = (NG - 1) * 4 + ncg
                        sync.wait_ge(s_int.h, 4 * (ncc + 1))
                        s_ist.dinc(sync.dma_start(
                            out=inter_s[P * ncc:P * (ncc + 1), :],
                            in_=intb[ncg % 2][:]))

                @block.vector
                def _(v):
                    def d4_for(i2, wfloor):
                        emit_decode_d3(v, i2)

                    def evac(g1, j):
                        eidx = g1 * 16 + j
                        ncg, ts = divmod(j, 4)
                        ncc = g1 * 4 + ncg
                        v.wait_ge(s_gmm.h, eidx + 1)
                        if ncc >= 2:
                            v.wait_ge(s_ist.h, 16 * (ncc - 1))
                        s_upc.inc(v.tensor_copy(uub[0][:], pb[4 + ts][:]))
                        v.wait_ge(s_gel.h, eidx + 1)
                        s_int.inc(v.tensor_mul(
                            intb[ncg % 2][:, 512 * ts:512 * (ts + 1)],
                            ggb[eidx % 2][:], uub[0][:]))

                    for g in range(NG):
                        for j in range(16):
                            i = g * 16 + j
                            if j >= 2:
                                d4_for(i - 2, 0)
                            emit_decode_d1(v, i)
                            if g >= 1:
                                evac(g - 1, j)
                        d4_for(g * 16 + 14, 0)
                        d4_for(g * 16 + 15, 0)
                    for j in range(16):
                        evac(NG - 1, j)

                @block.gpsimd
                def _(gp):
                    for i in range(NG * 16):
                        g2, m2_, c2, _, _ = p4_tile_params(i)
                        wdst = wbuf[m2_][g2 % 2][:, 1024 * c2:1024 * (c2 + 1)]
                        emit_decode_d4(gp, i, wdst,
                                       (s_gmm, 16 * (g2 - 1)) if g2 >= 2 else None)

                @block.tensor
                def _(tn):
                    tn.wait_ge(s_x2t.h, 256)
                    for g in range(NG):
                        tn.wait_ge(s_d4.h, 16 * (g + 1))
                        for ncg in range(4):
                            for ts in range(4):
                                eidx = g * 16 + ncg * 4 + ts
                                if eidx >= 4:
                                    tn.wait_ge(s_gel.h, eidx - 3)
                                    tn.wait_ge(s_upc.h, eidx - 3)
                                for k in range(16):
                                    tn.matmul(
                                        pb[ts][:],
                                        wbuf[0][g % 2][:, 512 * k + P * ncg:
                                                       512 * k + P * (ncg + 1)],
                                        x2t[:, 2048 * k + 512 * ts:
                                            2048 * k + 512 * (ts + 1)],
                                        start=(k == 0), stop=(k == 15))
                                for k in range(16):
                                    ins = tn.matmul(
                                        pb[4 + ts][:],
                                        wbuf[1][g % 2][:, 512 * k + P * ncg:
                                                       512 * k + P * (ncg + 1)],
                                        x2t[:, 2048 * k + 512 * ts:
                                            2048 * k + 512 * (ts + 1)],
                                        start=(k == 0), stop=(k == 15))
                                s_gmm.inc(ins)
            assert s_gmm.v == NG * 16 and s_d4.v == NG * 16
            assert s_int.v == NG * 16 and s_ist.v == 16 * NG * 4

        if PHASE_LIMIT >= 5:
            if "keep_p4" not in DEV_SKIP:
                stP4.close()
            accq = stB.enter_context(nc.sbuf_tensor("accq", [P, 4 * TOK], F32))
            ib = [stB.enter_context(nc.sbuf_tensor(f"ib{i}", [P, 16 * 512], BF16))
                  for i in range(2)]
            ybuf5 = [stB.enter_context(nc.sbuf_tensor(f"ybuf5{i}", [P, 512], F32))
                     for i in range(2)]
            finb = [stB.enter_context(nc.sbuf_tensor(f"finb{i}", [P, 512], F32))
                    for i in range(2)]

            P4_DT = NG * 16 if PHASE_LIMIT >= 4 else 0
            NB = 8

            with nc.Block("p5") as block:
                @block.sync
                def _(sync):
                    if PHASE_LIMIT >= 4:
                        sync.wait_ge(s_gmm.h, 512)
                    for fq in range(4):
                        for b in range(NB):
                            bfq = fq * NB + b
                            for jc in range(8):
                                emit_decode_sync(
                                    sync, bt_d, sc_d,
                                    1024 * b + 128 * jc, 512 * fq)
                                DT[0] += 1
                            for ts in range(4):
                                tsi = bfq * 4 + ts
                                if tsi >= 2:
                                    sync.wait_ge(s_dmm.h, 4 * (tsi - 1))
                                for ncb in range(16):
                                    s_ild.dinc(sync.dma_start(
                                        out=ib[tsi % 2][:, 512 * ncb:512 * (ncb + 1)],
                                        in_=inter_s[2048 * b + P * ncb:
                                                    2048 * b + P * (ncb + 1),
                                                    512 * ts:512 * (ts + 1)]))
                        for ti in range(16):
                            fti = fq * 16 + ti
                            if fti >= 2:
                                sync.wait_ge(s_fin.h, 4 * (fti - 1))
                            s_yld.dinc(sync.dma_start(
                                out=ybuf5[ti % 2][:],
                                in_=out[P * ti:P * (ti + 1),
                                        512 * fq:512 * (fq + 1)]))
                            sync.wait_ge(s_fin.h, 4 * (fti + 1))
                            s_ost.dinc(sync.dma_start(
                                out=out[P * ti:P * (ti + 1),
                                        512 * fq:512 * (fq + 1)],
                                in_=finb[ti % 2][:]))
                    sync.wait_ge(s_ost.h, s_ost.v)

                @block.scalar
                def _(sc):
                    for fq in range(4):
                        for b in range(NB):
                            for jc in range(8):
                                emit_decode_d2(sc, P4_DT + (fq * NB + b) * 8 + jc)

                @block.vector
                def _(v):
                    base = P4_DT

                    def d4_for5(r2):
                        emit_decode_d3(v, base + r2)

                    for fq in range(4):
                        for b in range(NB):
                            bfq = fq * NB + b
                            for jc in range(8):
                                i = base + bfq * 8 + jc
                                if jc >= 2:
                                    d4_for5(bfq * 8 + jc - 2)
                                emit_decode_d1(v, i)
                            d4_for5(bfq * 8 + 6)
                            d4_for5(bfq * 8 + 7)
                            for ts in range(4):
                                for f in range(4):
                                    didx = (bfq * 4 + ts) * 4 + f
                                    v.wait_ge(s_dmm.h, didx + 1)
                                    dst = accq[:, 2048 * f + 512 * ts:
                                               2048 * f + 512 * (ts + 1)]
                                    if b == 0:
                                        s_acc.inc(v.tensor_copy(
                                            dst, pb[4 + f][:]))
                                    else:
                                        s_acc.inc(v.tensor_add(
                                            dst, dst, pb[4 + f][:]))
                        for ti in range(16):
                            fti = fq * 16 + ti
                            for fc in range(4):
                                tf = fti * 4 + fc
                                v.wait_ge(s_tp.h, tf + 1)
                                v.wait_ge(s_yld.h, 16 * (fti + 1))
                                if fti >= 2:
                                    v.wait_ge(s_ost.h, 16 * (fti - 1))
                                s_fin.inc(v.tensor_add(
                                    finb[ti % 2][:, 128 * fc:128 * (fc + 1)],
                                    pb[6 + fc % 2][:, 0:P],
                                    ybuf5[ti % 2][:, 128 * fc:128 * (fc + 1)]))

                @block.gpsimd
                def _(gp):
                    for i in range(P4_DT, DT[0]):
                        r2 = i - P4_DT
                        bfq2, jc2 = divmod(r2, 8)
                        wdst = wbuf[0][bfq2 % 2][:, 1024 * jc2:1024 * (jc2 + 1)]
                        if bfq2 >= 2:
                            wft = (s_dmm, 16 * (bfq2 - 1))
                        elif PHASE_LIMIT >= 4:
                            wft = (s_gmm, 512)
                        else:
                            wft = None
                        emit_decode_d4(gp, i, wdst, wft)

                @block.tensor
                def _(tn):
                    for fq in range(4):
                        for b in range(NB):
                            bfq = fq * NB + b
                            tn.wait_ge(s_d4.h, P4_DT + 8 * (bfq + 1))
                            if b == 0 and fq >= 1:
                                tn.wait_ge(s_fin.h, 64 * fq)
                            for ts in range(4):
                                tsi = bfq * 4 + ts
                                tn.wait_ge(s_ild.h, 16 * 16 * (tsi + 1))
                                for f in range(4):
                                    didx = tsi * 4 + f
                                    if didx >= 4:
                                        tn.wait_ge(s_acc.h, didx - 3)
                                    for ncb in range(16):
                                        ins = tn.matmul(
                                            pb[4 + f][:],
                                            wbuf[0][bfq % 2][:, 512 * ncb + P * f:
                                                             512 * ncb + P * (f + 1)],
                                            ib[tsi % 2][:, 512 * ncb:512 * (ncb + 1)],
                                            start=(ncb == 0), stop=(ncb == 15))
                                    s_dmm.inc(ins)
                        tn.wait_ge(s_acc.h, 128 * (fq + 1))
                        for ti in range(16):
                            fti = fq * 16 + ti
                            for fc in range(4):
                                tf = fti * 4 + fc
                                if tf >= 2:
                                    tn.wait_ge(s_fin.h, tf - 1)
                                s_tp.inc(tn.transpose(
                                    pb[6 + fc % 2][:, 0:P],
                                    accq[:, 2048 * fc + P * ti:
                                         2048 * fc + P * (ti + 1)],
                                    sb_ident[:]))

    return nc



def _host_prep(inputs):
    x = np.asarray(inputs["L0_attn_hidden_states"])
    Wq = np.asarray(inputs["L0_attn_Wq"])
    bq = np.asarray(inputs["L0_attn_bq"])
    Wk = np.asarray(inputs["L0_attn_Wk"])
    bk = np.asarray(inputs["L0_attn_bk"])
    Wv = np.asarray(inputs["L0_attn_Wv"])
    bv = np.asarray(inputs["L0_attn_bv"])
    Wo = np.asarray(inputs["L0_attn_Wo"])
    gW = np.asarray(inputs["L0_ffn_gate_W_packed"])
    gS = np.asarray(inputs["L0_ffn_gate_scales"])
    uW = np.asarray(inputs["L0_ffn_up_W_packed"])
    uS = np.asarray(inputs["L0_ffn_up_scales"])
    dW = np.asarray(inputs["L0_ffn_down_W_packed"])
    dS = np.asarray(inputs["L0_ffn_down_scales"])

    pi, pin = PI_DM, PI_MLP

    wqT_ = np.ascontiguousarray(Wq.T[pi, :])
    wkT_ = np.ascontiguousarray(Wk.T[pi, :])
    wvT_ = np.ascontiguousarray(Wv.T[pi, :])
    woT_ = np.ascontiguousarray(Wo.T[:, pi])
    bq_cols = np.ascontiguousarray(bq.reshape(NH, P).T)
    bk_rep = np.ascontiguousarray(np.broadcast_to(bk, (P, KV)))
    bv_rep = np.ascontiguousarray(np.broadcast_to(bv, (P, KV)))

    def pack_bytes(Wp):
        return np.ascontiguousarray(Wp.view(np.uint8)[:, ::4])

    bt_g = np.ascontiguousarray(pack_bytes(gW).T[:, pin])
    bt_u = np.ascontiguousarray(pack_bytes(uW).T[:, pin])
    bt_d = np.ascontiguousarray(pack_bytes(dW).T[:, pi])

    def rep_scales(S):
        return np.repeat(np.ascontiguousarray(S.T), 16, axis=0)

    sc_g_ = np.ascontiguousarray(rep_scales(gS)[:, pin]).astype(np.float32)
    sc_u_ = np.ascontiguousarray(rep_scales(uS)[:, pin]).astype(np.float32)
    sc_d_ = np.ascontiguousarray(rep_scales(dS)[:, pi]).astype(np.float32)

    ident = np.eye(P, dtype=np.float32)
    import ml_dtypes
    eh = np.zeros((16, NH * P), dtype=ml_dtypes.bfloat16)
    for h in range(NH):
        eh[h, P * h:P * (h + 1)] = 1.0

    shared = dict(wqT=wqT_, wkT=wkT_, wvT=wvT_, woT=woT_, bq_cols=bq_cols,
                  bk_rep=bk_rep, bv_rep=bv_rep, bt_g=bt_g, bt_u=bt_u,
                  bt_d=bt_d, sc_g=sc_g_, sc_u=sc_u_, sc_d=sc_d_,
                  ident_f32=ident, eh_onehot=eh)

    in_maps = []
    for c in range(NCORE):
        b, h = c // 2, c % 2
        own = np.ascontiguousarray(x[b, 2048 * h:2048 * (h + 1), :][:, pi])
        oth = np.ascontiguousarray(x[b, 2048 * (1 - h):2048 * (2 - h), :][:, pi])
        m = dict(shared)
        m["x_full"] = np.ascontiguousarray(
            np.concatenate([own, oth], axis=0)).astype(np.float32, copy=False)
        in_maps.append(m)
    return in_maps


def kernel(**inputs):
    in_maps = _host_prep(inputs)
    nc = build_program()
    res = run_bass_kernel_spmd(nc, in_maps, core_ids=list(range(NCORE)))
    outs = [np.asarray(r["out"]) for r in res.results]
    inv = np.empty_like(PI_DM)
    inv[PI_DM] = np.arange(DM)
    full = np.empty((4, SEQ, DM), dtype=np.float32)
    for c in range(NCORE):
        b, h = c // 2, c % 2
        full[b, 2048 * h:2048 * (h + 1), :] = outs[c][:, inv]
    return full
